# version 01
# baseline (speedup 1.0000x reference)
import os
import numpy as np
import ml_dtypes
import concourse.bass as bass
import concourse.mybir as mybir
from concourse.bass_utils import run_bass_kernel_spmd

F32 = mybir.dt.float32
BF16 = mybir.dt.bfloat16
AF = mybir.ActivationFunctionType
ALU = mybir.AluOpType
AX = mybir.AxisListType

NCORES = 8
D, T = 1024, 2048
DC = 8
NTCH, TCH = 4, 512
SUB = 128
DIN, DICH = 2048, 16
NH, HD, DST = 32, 64, 128
FF, FFC = 4096, 32
NEXP, NAH = 8, 16
BIGNEG = -1e30
BF = ml_dtypes.bfloat16

RING_N = 6
RING_W = 2048

ENGS = ("tensor", "vector", "scalar", "sync", "gpsimd")


class Bld:
    def __init__(self, nc):
        self.nc = nc
        self.q = {e: [] for e in ENGS}
        self.n = {}
        self.H = {}
        self.ring_load = 0
        self.sb = {}
        self.ps = {}

    def inc(self, name, amt=1):
        self.n[name] = self.n.get(name, 0) + amt
        return (name, amt)

    def val(self, name):
        return self.n.get(name, 0)

    SIMSAFE = bool(os.environ.get("SIMSAFE"))

    def op(self, eng, fn, waits=(), incs=()):
        for (s, v) in waits:
            assert self.n.get(s, 0) >= v, f"wait on future {s}>={v} have {self.n.get(s,0)}"
        waits = list(waits)
        incs = list(incs)
        if eng in ("sync", "gpsimd"):
            for (s, amt) in incs:
                if amt >= 16:
                    waits.append((s, self.val(s) - amt))
        if self.SIMSAFE and eng in ("vector", "scalar", "tensor"):
            if not hasattr(self, "last_eng_sem"):
                self.last_eng_sem = {}
            prev = self.last_eng_sem.get(eng)
            if prev is not None:
                waits.append(prev)
            if not incs:
                incs.append(self.inc("own_" + eng, 1))
            self.last_eng_sem[eng] = (incs[0][0], self.val(incs[0][0]))
        w = tuple((s, v) for (s, v) in waits if v > 0)
        i = tuple(incs)

        def run(e, w=w, i=i, fn=fn):
            for (s, v) in w:
                e.wait_ge(self.H[s], v)
            inst = fn(e)
            first = True
            for (s, amt) in i:
                if inst is not None and first:
                    inst.then_inc(self.H[s], amt)
                    first = False
                else:
                    e.sem_inc(self.H[s], amt)
        self.q[eng].append(run)

    def wload(self, kw_fn, waits=()):
        i = self.ring_load
        k = i % RING_N
        wts = list(waits)
        if i >= RING_N:
            wts.append((f"wf{k}", i // RING_N))
        if getattr(self, "dump_sync", False):
            wts.append(("xg", self.val("xg")))
        slot = self.sb[f"wring{k}"]
        inc = self.inc(f"ws{k}", 16)
        self.op("gpsimd", (lambda s, f: lambda e: e.dma_start(**f(s)))(slot, kw_fn),
                waits=wts, incs=[inc])
        self.ring_load += 1
        return slot, (f"ws{k}", self.val(f"ws{k}")), f"wf{k}"

    def flush(self, block):
        @block.tensor
        def _(e):
            for f in self.q["tensor"]:
                f(e)

        @block.vector
        def _(e):
            for f in self.q["vector"]:
                f(e)

        @block.scalar
        def _(e):
            for f in self.q["scalar"]:
                f(e)

        @block.sync
        def _(e):
            for f in self.q["sync"]:
                f(e)

        @block.gpsimd
        def _(e):
            for f in self.q["gpsimd"]:
                f(e)


class Tiles:
    def __init__(self, arena, off, n, w):
        self.a, self.off, self.n, self.w = arena, off, n, w

    def __getitem__(self, i):
        assert 0 <= i < self.n
        o = self.off + i * self.w
        return self.a[:, o:o + self.w]

    def sl(self, i, c0, c1):
        o = self.off + i * self.w
        return self.a[:, o + c0:o + c1]


def _t(a):
    return np.ascontiguousarray(a)


def host_consts():
    c = {}
    c["identb"] = np.eye(128, dtype=BF)
    iu = (np.triu(np.ones((128, 128), np.float32), 1) * BIGNEG)
    c["upA"] = _t(iu.astype(BF))
    c["upS"] = _t((np.tril(np.ones((128, 128), np.float32), -1) * (-BIGNEG)).astype(BF))
    mb = np.zeros((128, 4 * 512), np.float32)
    for j in range(4):
        a = np.arange(128)[:, None] + j * 128
        cq = np.arange(512)[None, :]
        mb[:, j * 512:(j + 1) * 512] = (a > cq) * BIGNEG
    c["maskB"] = _t(mb.astype(BF))
    inv = 1.0 / (10000.0 ** (np.arange(0, HD, 2, dtype=np.float64) / HD))
    fr = np.arange(T, dtype=np.float64)[:, None] * inv[None, :]
    emb = np.concatenate([fr, fr], axis=1)
    cos = np.cos(emb).T.astype(np.float32)
    sin = np.sin(emb).T.astype(np.float32)
    c["ropeC"] = _t(np.concatenate([cos, cos], axis=0).astype(BF))
    c["ropeS"] = _t(np.concatenate([sin, sin], axis=0).astype(BF))
    rp = np.zeros((128, 128), np.float32)
    for base in (0, 64):
        for i in range(32):
            rp[base + i + 32, base + i] = -1.0
            rp[base + i, base + i + 32] = 1.0
    c["rotP"] = _t(rp)
    oh = np.zeros((8, 1024), np.float32)
    for e in range(8):
        oh[e, e * 128:(e + 1) * 128] = 1.0
    c["OH8"] = _t(oh)
    so = np.zeros((128, 2), np.float32)
    so[:64, 0] = 1.0
    so[64:, 1] = 1.0
    c["sel64"] = _t(so)
    sb_ = np.zeros((2, 128), np.float32)
    sb_[0, :64] = 1.0
    sb_[1, 64:] = 1.0
    c["bc64"] = _t(sb_)
    return c


def host_prep(inputs):
    w = {}
    g = lambda k: inputs[k]
    w["w_in"] = _t(g("L0_attn_a_W_in"))
    w["conv_w"] = _t(g("L0_attn_a_conv_w").reshape(18, 128, 4).transpose(1, 0, 2).reshape(128, 72))
    w["conv_b"] = _t(g("L0_attn_a_conv_b").reshape(18, 128).T)
    w["A_log"] = _t(g("L0_attn_a_A_log").reshape(32, 1))
    w["Dm"] = _t(g("L0_attn_a_Dm").reshape(32, 1))
    w["dt_bias"] = _t(g("L0_attn_a_dt_bias").reshape(32, 1))
    w["mnw"] = _t(g("L0_attn_a_mnorm_w").reshape(16, 128).T)
    w["w_out"] = _t(g("L0_attn_a_W_out"))
    w["w_qkv"] = _t(g("L0_attn_a_W_qkv"))
    w["w_cproj"] = _t(g("L0_attn_a_W_cproj"))
    qw, qb = g("L0_attn_a_qconv_w"), g("L0_attn_a_qconv_b")
    w["qcw"] = _t(np.concatenate([qw, qw], axis=0))
    w["qcb"] = _t(np.concatenate([qb, qb]).reshape(128, 1))
    w["kvw"] = _t(np.concatenate([g("L0_attn_a_kconv_w"), g("L0_attn_a_vconv_w")], axis=0))
    w["kvb"] = _t(np.concatenate([g("L0_attn_a_kconv_b"), g("L0_attn_a_vconv_b")]).reshape(128, 1))
    w["maa_k"] = _t(g("L0_attn_a_time_maa_k").reshape(8, 128).T)
    w["maa_r"] = _t(g("L0_attn_a_time_maa_r").reshape(8, 128).T)
    w["w_key"] = _t(g("L0_attn_a_W_key"))
    w["w_rec"] = _t(g("L0_attn_a_W_rec"))
    w["w_val"] = _t(g("L0_attn_a_W_val"))
    w["Wq"], w["Wk"], w["Wv"], w["Wo"] = (_t(g("L0_ffn_" + k)) for k in ("Wq", "Wk", "Wv", "Wo"))
    for k in ("bq", "bk", "bv", "bo"):
        w[k] = _t(g("L0_ffn_" + k).reshape(8, 128).T)
    w["Wg"] = _t(g("L0_ffn_Wg"))
    w["bg"] = _t(g("L0_ffn_bg").reshape(8, 1))
    w["W1"] = _t(g("L0_ffn_W1").reshape(8192, 4096))
    w["b1"] = _t(g("L0_ffn_b1").reshape(8, 32, 128).transpose(2, 0, 1).reshape(128, 256))
    w["W2"] = _t(g("L0_ffn_W2").reshape(8, 32, 128, 8, 128).transpose(0, 3, 2, 1, 4).reshape(8192, 4096))
    w["b2"] = _t(g("L0_ffn_b2"))
    for k in ("g1", "be1", "g2", "be2"):
        w[k] = _t(g("L0_ffn_" + k).reshape(8, 128).T)
    Wqkvb = g("L0_attn_b_W_qkvb")
    idx = np.zeros(3088, np.int64)
    ar = np.arange(64)
    for h in range(16):
        base = h * 193
        idx[h * 64:(h + 1) * 64] = base + ar
        idx[1024 + h * 64:1024 + (h + 1) * 64] = base + 64 + ar
        idx[2048 + h * 64:2048 + (h + 1) * 64] = base + 128 + ar
        idx[3072 + h] = base + 192
    w["w_qkvb"] = _t(Wqkvb[idx].T)
    w["w_ob"] = _t(g("L0_attn_b_W_o").T)
    w["lng"] = _t(g("L0_attn_b_ln_g").reshape(8, 128).T)
    w["lnb"] = _t(g("L0_attn_b_ln_b").reshape(8, 128).T)
    return w


CONST_SPECS = [
    ("identb", "identb", [128, 128], BF16),
    ("upA", "upA", [128, 128], BF16),
    ("upS", "upS", [128, 128], BF16),
    ("maskB", "maskB", [128, 2048], BF16),
    ("rotP", "rotP", [128, 128], F32),
    ("OH8", "OH8", [8, 1024], F32),
    ("sel64", "sel64", [128, 2], F32),
    ("bc64", "bc64", [2, 128], F32),
    ("cw", "conv_w", [128, 72], F32),
    ("cb", "conv_b", [128, 18], F32),
    ("A_log", "A_log", [32, 1], F32),
    ("Dm", "Dm", [32, 1], F32),
    ("dtb", "dt_bias", [32, 1], F32),
    ("mnw", "mnw", [128, 16], F32),
    ("qcw", "qcw", [128, 3], F32),
    ("qcb", "qcb", [128, 1], F32),
    ("kvw", "kvw", [128, 3], F32),
    ("kvb", "kvb", [128, 1], F32),
    ("maa_k", "maa_k", [128, 8], F32),
    ("maa_r", "maa_r", [128, 8], F32),
    ("bq", "bq", [128, 8], F32),
    ("bk", "bk", [128, 8], F32),
    ("bv", "bv", [128, 8], F32),
    ("bo", "bo", [128, 8], F32),
    ("bg", "bg", [8, 1], F32),
    ("b1t", "b1", [128, 256], F32),
    ("b2b", "b2", [8, 1024], F32),
    ("g1", "g1", [128, 8], F32),
    ("be1", "be1", [128, 8], F32),
    ("g2", "g2", [128, 8], F32),
    ("be2", "be2", [128, 8], F32),
    ("lng", "lng", [128, 8], F32),
    ("lnb", "lnb", [128, 8], F32),
]


class Prog:

    def __init__(self, dev_stop=None, launch=1):
        self.dev_stop = dev_stop
        self.launch = launch
        nc = bass.Bass("TRN2", target_bir_lowering=False, debug=False)
        self.nc = nc
        self.b = Bld(nc)
        self.b.dump_sync = (dev_stop == "wout0")
        self.d = {}
        dr = lambda name, shape, dt=F32, kind=None: self.d.__setitem__(
            name, nc.dram_tensor(name, shape, dt, kind=kind) if kind else nc.dram_tensor(name, shape, dt))
        dr("x_t", [D, T], F32, "ExternalInput")
        for name, hk, shape, dt in CONST_SPECS:
            dr("c_" + name, shape, dt, "ExternalInput")
        dr("ropeC", [128, T], BF16, "ExternalInput")
        dr("ropeS", [128, T], BF16, "ExternalInput")
        dr("identf", [128, 128], F32, "ExternalInput")
        for name, shape in [
            ("w_in", [D, 4384]), ("w_out", [DIN, D]), ("w_qkv", [D, 1152]),
            ("w_cproj", [D, D]), ("w_key", [D, FF]), ("w_rec", [D, D]),
            ("w_val", [FF, D]), ("Wq", [D, D]), ("Wk", [D, D]), ("Wv", [D, D]),
            ("Wo", [D, D]), ("Wg", [D, 8]), ("W1", [NEXP * D, FF]),
            ("W2", [NEXP * D, FF]), ("w_qkvb", [D, 3088]),
        ]:
            dr(name, shape, F32, "ExternalInput")
        if launch == 3:
            dr("x3_in", [D, T], F32, "ExternalInput")
            dr("h3_in", [D, T], F32, "ExternalInput")
            dr("lo_in", [D, T], F32, "ExternalInput")
            dr("w_ob", [D, D], F32, "ExternalInput")
        dr("dU", [D, T], F32)
        dr("dX1", [D, T], F32)
        dr("dT1", [D, T], F32)
        dr("o_x", [D, T], F32, "ExternalOutput")
        dr("o_h", [D, T], F32, "ExternalOutput")
        dr("o_q", [3088, T], F32, "ExternalOutput")
        if dev_stop:
            dr("o_dev", [4608, T], F32, "ExternalOutput")

    def alloc(self, ctx):
        nc, b = self.nc, self.b
        b.AB = ctx.enter_context(nc.sbuf_tensor("AB", [128, 71680], BF16))
        b.AF = ctx.enter_context(nc.sbuf_tensor("AF", [128, 8832], F32))
        for i in range(RING_N):
            b.sb[f"wring{i}"] = b.AB[:, i * RING_W:(i + 1) * RING_W]
        self.PH = RING_N * RING_W
        for name, hk, shape, dt in CONST_SPECS:
            b.sb[name] = ctx.enter_context(nc.sbuf_tensor("t_" + name, shape, dt))
        for name, shape, dt in [("ropeC", [128, T], BF16), ("ropeS", [128, T], BF16),
                                ("identf", [128, 128], F32)]:
            b.sb[name] = ctx.enter_context(nc.sbuf_tensor("t_" + name, shape, dt))
        smalls = [
            ("onesf", [128, 1], F32), ("onesb", [128, 1], BF16), ("ones1", [128, 128], F32),
            ("ones1b", [1, 128], BF16),
            ("eps5", [1, 1], F32), ("eps6", [1, 1], F32),
            ("R4b", [128, 1536], F32),
            ("nbig", [128, 1], F32),
            ("carry", [128, 72], F32),
            ("erow", [1, 128], F32),
            ("z32", [32, 128], F32),
            ("expA", [32, 1], F32),
            ("Drep", [128, 16], F32),
            ("esc", [128, 1], F32),
        ]
        for name, shape, dt in smalls:
            b.sb[name] = ctx.enter_context(nc.sbuf_tensor("s_" + name, shape, dt))
        for nm, shape, dt in [("pmm0", [128, 512], F32), ("pmm1", [128, 512], F32),
                              ("pmm2", [128, 512], F32), ("pmm3", [128, 512], F32),
                              ("py0", [128, 512], F32), ("py1", [128, 512], F32),
                              ("pst", [128, 512], F32), ("ptr", [128, 256], BF16)]:
            b.ps[nm] = ctx.enter_context(nc.psum_tensor(nm, shape, dt))
        ring_sems = [f"ws{k}" for k in range(RING_N)] + [f"wf{k}" for k in range(RING_N)]
        own_sems = ["own_vector", "own_scalar", "own_tensor"] if Bld.SIMSAFE else []
        for s in ["ld", "tm", "vm", "sm", "gm", "xs", "xg", "so"] + ring_sems + own_sems:
            b.H[s] = ctx.enter_context(nc.semaphore("sem_" + s))
            b.n.setdefault(s, 0)
        self.bankg = {nm: ("vm", 0) for nm in self.b.ps}

    def ABt(self, off, n, w):
        return Tiles(self.b.AB, off, n, w)

    def AFt(self, off, n, w):
        return Tiles(self.b.AF, off, n, w)

    def bank_wait(self, nm):
        s, v = self.bankg[nm]
        return [(s, v)]

    def bank_set(self, nm, sem):
        self.bankg[nm] = (sem, self.b.val(sem))

    def emit_consts(self):
        b, d = self.b, self.d
        for name, hk, shape, dt in CONST_SPECS:
            t = b.sb[name]
            b.op("sync", (lambda ap, tt: lambda e: e.dma_start(out=tt[:], in_=ap.ap()))(d["c_" + name], t),
                 incs=[b.inc("ld", 16)])
        for name in ("ropeC", "ropeS", "identf"):
            t = b.sb[name]
            b.op("sync", (lambda ap, tt: lambda e: e.dma_start(out=tt[:], in_=ap.ap()))(d[name], t),
                 incs=[b.inc("ld", 16)])
        b.op("sync", lambda e: e.dma_start(out=b.sb["Drep"][0:64, :],
                                           in_=bass.AP(d["c_Dm"], 0, [[0, 64], [2, 16]]),
                                           allow_slow_non_contiguous=True),
             incs=[b.inc("ld", 16)])
        b.op("sync", lambda e: e.dma_start(out=b.sb["Drep"][64:128, :],
                                           in_=bass.AP(d["c_Dm"], 1, [[0, 64], [2, 16]]),
                                           allow_slow_non_contiguous=True),
             incs=[b.inc("ld", 16)])
        LD = b.val("ld")
        for nm, v in [("onesf", 1.0), ("onesb", 1.0), ("ones1", 1.0), ("ones1b", 1.0),
                      ("eps5", 1e-5), ("eps6", 1e-6), ("z32", 0.0), ("carry", 0.0), ("nbig", -1e30)]:
            b.op("vector", (lambda tt, vv: lambda e: e.memset(tt[:], vv))(b.sb[nm], v))
        b.op("scalar", lambda e: e.activation(b.sb["expA"][:], b.sb["A_log"][:], AF.Exp,
                                              bias=0.0, scale=1.0),
             waits=[("ld", LD)], incs=[b.inc("sm", 1)])
        b.op("vector", lambda e: None, waits=[("ld", LD)], incs=[b.inc("vm", 1)])
        self.LD = LD
        self.init_vm = b.val("vm")

    def emit_boundary(self, xin, ufin, xout, uout, eps_tile="eps5", bf_dst=None,
                      extra_in=None, gain=None, bias=None, post_add=None):
        b = self.b
        AFT = self.AFt
        xa = AFT(0, 2, 512)
        xu = AFT(1024, 2, 512)
        sq = AFT(2048, 1, 512)
        have_sum = ufin is not None or extra_in is not None
        tr0 = b.sb["R4b"][0:1, 0:512]
        tr1 = b.sb["R4b"][0:1, 512:1024]
        tr2 = b.sb["R4b"][0:1, 1024:1536]
        for tc in range(NTCH):
            csl = slice(tc * TCH, (tc + 1) * TCH)
            for dc in range(DC):
                rsl = slice(dc * 128, (dc + 1) * 128)
                ta = xa[dc % 2]
                b.op("sync", (lambda a, t: lambda e: e.dma_start(out=t, in_=a))(xin[rsl, csl], ta),
                     waits=[("vm", b.val("vm")), ("tm", b.val("tm"))], incs=[b.inc("xs", 16)])
                if ufin is not None:
                    tu = xu[dc % 2]
                    b.op("sync", (lambda a, t: lambda e: e.dma_start(out=t, in_=a))(ufin[rsl, csl], tu),
                         incs=[b.inc("xs", 16)])
                    b.op("vector", (lambda o, i0, i1: lambda e: e.tensor_tensor(o, i0, i1, ALU.add))(ta, ta, tu),
                         waits=[("xs", b.val("xs"))], incs=[b.inc("vm", 1)])
                if extra_in is not None:
                    tu = xu[dc % 2]
                    b.op("sync", (lambda a, t: lambda e: e.dma_start(out=t, in_=a))(extra_in[rsl, csl], tu),
                         waits=[("vm", b.val("vm"))], incs=[b.inc("xs", 16)])
                    b.op("vector", (lambda o, i0, i1: lambda e: e.tensor_tensor(o, i0, i1, ALU.add))(ta, ta, tu),
                         waits=[("xs", b.val("xs"))], incs=[b.inc("vm", 1)])
                if have_sum:
                    b.op("sync", (lambda a, t: lambda e: e.dma_start(out=a, in_=t))(xout[rsl, csl], ta),
                         waits=[("vm", b.val("vm"))], incs=[b.inc("xs", 16)])
                wv = [("vm", b.val("vm"))] if have_sum else [("xs", b.val("xs"))]
                b.op("vector", (lambda o, i: lambda e: e.tensor_tensor(o, i, i, ALU.mult))(sq[0], ta),
                     waits=wv + [("tm", b.val("tm"))], incs=[b.inc("vm", 1)])
                vm = b.val("vm")
                b.op("tensor", (lambda o, s, m, st, sp: lambda e: e.matmul(o, s, m, start=st, stop=sp))(
                    self.b.ps["py0"][0:1, :], b.sb["onesf"][:], ta, dc == 0, dc == DC - 1),
                    waits=[("vm", vm)] + (self.bank_wait("py0") if dc == 0 else []),
                    incs=[b.inc("tm", 1)])
                b.op("tensor", (lambda o, s, m, st, sp: lambda e: e.matmul(o, s, m, start=st, stop=sp))(
                    self.b.ps["py1"][0:1, :], b.sb["onesf"][:], sq[0], dc == 0, dc == DC - 1),
                    waits=[("vm", vm)] + (self.bank_wait("py1") if dc == 0 else []),
                    incs=[b.inc("tm", 1)])
            tmv = b.val("tm")
            b.op("vector", (lambda o, i: lambda e: e.tensor_scalar(o, i, 1.0 / D, None, ALU.mult))(
                tr0, self.b.ps["py0"][0:1, :]), waits=[("tm", tmv)], incs=[b.inc("vm", 1)])
            b.op("vector", (lambda o, i: lambda e: e.tensor_scalar(o, i, 1.0 / D, None, ALU.mult))(
                tr1, self.b.ps["py1"][0:1, :]), waits=[("tm", tmv)], incs=[b.inc("vm", 1)])
            b.op("vector", (lambda o, i: lambda e: e.tensor_tensor(o, i, i, ALU.mult))(tr2, tr0),
                 incs=[b.inc("vm", 1)])
            b.op("vector", (lambda o, a2, b2: lambda e: e.tensor_tensor(o, a2, b2, ALU.subtract))(tr1, tr1, tr2),
                 incs=[b.inc("vm", 1)])
            self.bank_set("py0", "vm")
            self.bank_set("py1", "vm")
            b.op("scalar", (lambda o, i, ep: lambda e: e.activation(o, i, AF.Ln, bias=ep, scale=1.0))(
                tr2, tr1, b.sb[eps_tile][:]),
                waits=[("vm", b.val("vm"))], incs=[b.inc("sm", 1)])
            b.op("scalar", (lambda o, i: lambda e: e.activation(o, i, AF.Exp, bias=0.0, scale=-0.5))(
                tr1, tr2), incs=[b.inc("sm", 1)])
            smv = b.val("sm")
            b.op("tensor", (lambda o, s, m: lambda e: e.matmul(o, s, m, start=True, stop=True))(
                self.b.ps["pst"][:], b.sb["ones1"][0:1, :], tr0),
                waits=[("sm", smv)] + self.bank_wait("pst"), incs=[b.inc("tm", 1)])
            b.op("tensor", (lambda o, s, m: lambda e: e.matmul(o, s, m, start=True, stop=True))(
                self.b.ps["py0"][:], b.sb["ones1"][0:1, :], tr1),
                waits=[("sm", smv)] + self.bank_wait("py0"), incs=[b.inc("tm", 1)])
            tmv = b.val("tm")
            src2 = xout if have_sum else xin
            for dc in range(DC):
                rsl = slice(dc * 128, (dc + 1) * 128)
                ta = xa[dc % 2]
                b.op("sync", (lambda a, t: lambda e: e.dma_start(out=t, in_=a))(src2[rsl, csl], ta),
                     waits=[("vm", b.val("vm")), ("tm", b.val("tm"))], incs=[b.inc("xs", 16)])
                b.op("vector", (lambda o, i0, i1: lambda e: e.tensor_tensor(o, i0, i1, ALU.subtract))(
                    xu[dc % 2], ta, self.b.ps["pst"][:]),
                    waits=[("xs", b.val("xs")), ("tm", tmv)], incs=[b.inc("vm", 1)])
                b.op("vector", (lambda o, i0, i1: lambda e: e.tensor_tensor(o, i0, i1, ALU.mult))(
                    ta, xu[dc % 2], self.b.ps["py0"][:]),
                    incs=[b.inc("vm", 1)])
                if gain is not None:
                    b.op("vector", (lambda o, i0, s1, s2: lambda e: e.tensor_scalar(o, i0, s1, s2, ALU.mult, ALU.add))(
                        ta, ta, b.sb[gain][:, dc:dc + 1], b.sb[bias][:, dc:dc + 1]),
                        incs=[b.inc("vm", 1)])
                if post_add is not None:
                    b.op("sync", (lambda a, t: lambda e: e.dma_start(out=t, in_=a))(
                        post_add[rsl, csl], sq[0]),
                        waits=[("vm", b.val("vm"))], incs=[b.inc("xs", 16)])
                    b.op("vector", (lambda t: lambda e: e.tensor_tensor(t, t, sq[0], ALU.add))(ta),
                         waits=[("xs", b.val("xs"))], incs=[b.inc("vm", 1)])
                if bf_dst is not None:
                    b.op("vector", (lambda o, i: lambda e: e.tensor_copy(o, i))(
                        bf_dst.sl(dc, tc * TCH, tc * TCH + TCH), ta),
                        incs=[b.inc("vm", 1)])
                if uout is not None:
                    b.op("sync", (lambda a, t: lambda e: e.dma_start(out=a, in_=t))(uout[rsl, csl], ta),
                         waits=[("vm", b.val("vm"))], incs=[b.inc("xs", 16)])
            self.bank_set("pst", "vm")
            self.bank_set("py0", "vm")
        b.op("vector", lambda e: None, waits=[("xs", b.val("xs"))], incs=[b.inc("vm", 1)])

    def emit_rms_bf(self, src, bf_dst, eps_tile="eps6"):
        b = self.b
        AFT = self.AFt
        xa = AFT(0, 2, 512)
        sq = AFT(2048, 1, 512)
        tr0 = b.sb["R4b"][0:1, 0:512]
        tr1 = b.sb["R4b"][0:1, 512:1024]
        for tc in range(NTCH):
            csl = slice(tc * TCH, (tc + 1) * TCH)
            for dc in range(DC):
                rsl = slice(dc * 128, (dc + 1) * 128)
                ta = xa[dc % 2]
                b.op("sync", (lambda a, t: lambda e: e.dma_start(out=t, in_=a))(src[rsl, csl], ta),
                     waits=[("vm", b.val("vm")), ("tm", b.val("tm"))], incs=[b.inc("xs", 16)])
                b.op("vector", (lambda o, i: lambda e: e.tensor_tensor(o, i, i, ALU.mult))(sq[0], ta),
                     waits=[("xs", b.val("xs")), ("tm", b.val("tm"))], incs=[b.inc("vm", 1)])
                b.op("tensor", (lambda o, s, m, st, sp: lambda e: e.matmul(o, s, m, start=st, stop=sp))(
                    self.b.ps["py1"][0:1, :], b.sb["onesf"][:], sq[0], dc == 0, dc == DC - 1),
                    waits=[("vm", b.val("vm"))] + (self.bank_wait("py1") if dc == 0 else []),
                    incs=[b.inc("tm", 1)])
            tmv = b.val("tm")
            b.op("vector", (lambda o, i: lambda e: e.tensor_scalar(o, i, 1.0 / D, None, ALU.mult))(
                tr0, self.b.ps["py1"][0:1, :]), waits=[("tm", tmv)], incs=[b.inc("vm", 1)])
            self.bank_set("py1", "vm")
            b.op("scalar", (lambda o, i, ep: lambda e: e.activation(o, i, AF.Ln, bias=ep, scale=1.0))(
                tr1, tr0, b.sb[eps_tile][:]),
                waits=[("vm", b.val("vm"))], incs=[b.inc("sm", 1)])
            b.op("scalar", (lambda o, i: lambda e: e.activation(o, i, AF.Exp, bias=0.0, scale=-0.5))(
                tr0, tr1), incs=[b.inc("sm", 1)])
            b.op("tensor", (lambda o, s, m: lambda e: e.matmul(o, s, m, start=True, stop=True))(
                self.b.ps["pst"][:], b.sb["ones1"][0:1, :], tr0),
                waits=[("sm", b.val("sm"))] + self.bank_wait("pst"), incs=[b.inc("tm", 1)])
            tmv2 = b.val("tm")
            for dc in range(DC):
                rsl = slice(dc * 128, (dc + 1) * 128)
                ta = xa[dc % 2]
                b.op("sync", (lambda a, t: lambda e: e.dma_start(out=t, in_=a))(src[rsl, csl], ta),
                     waits=[("vm", b.val("vm"))], incs=[b.inc("xs", 16)])
                b.op("vector", (lambda o, i0, i1: lambda e: e.tensor_tensor(o, i0, i1, ALU.mult))(
                    bf_dst.sl(dc, tc * TCH, tc * TCH + TCH), ta, self.b.ps["pst"][:]),
                    waits=[("xs", b.val("xs")), ("tm", tmv2)], incs=[b.inc("vm", 1)])
            self.bank_set("pst", "vm")

    def emit_mamba(self, RH, u_d):
        b, d = self.b, self.d
        NTCH_l = int(os.environ.get("SSDTC", NTCH))
        NH_l = int(os.environ.get("SSDNH", NH))
        PH = self.PH
        XBCS = Tiles(b.AB, PH + 16384, 18, 512)
        ZS = Tiles(b.AB, PH + 16384 + 9216, 16, 512)
        Y16 = Tiles(b.AB, PH + 16384 + 9216 + 8192, 16, 512)
        YB = Tiles(b.AB, PH + 16384 + 9216 + 16384, 16, 512)
        sbase = PH + 16384 + 9216 + 24576
        Mp = b.AB[:, sbase:sbase + 128]
        XTb = b.AB[:, sbase + 128:sbase + 256]
        XTs = b.AB[:, sbase + 256:sbase + 320]
        BTs = b.AB[:, sbase + 320:sbase + 448]
        Cpp = b.AB[:, sbase + 448:sbase + 576]
        BTb = b.AB[:, sbase + 576:sbase + 704]
        Stile = Tiles(b.AB, sbase + 704, 32, 64)
        raw = self.AFt(0, 2, 520)
        cvt = self.AFt(1040, 2, 512)
        dtr = b.AF[0:32, 2064:2576]
        t32a = b.AF[0:32, 2576:3088]
        t32b = b.AF[0:32, 3088:3600]
        dtsp = b.AF[0:32, 3600:4112]
        nlog = b.AF[0:32, 4112:4624]
        Cc = b.AF[0:32, 0:128]
        TRS = b.AF[0:64, 128:256]
        CtDt = b.AF[:, 256:320]
        e1s = b.AF[:, 320:448]
        dcy = b.AF[:, 448:576]
        Cflat = b.AF[0:1, 4624:8720]
        sqv = b.AF[:, 576:1088]
        ps = b.ps

        for tc in range(NTCH_l):
            t0 = tc * TCH
            csl = slice(t0, t0 + TCH)
            silu_sm = {}
            for oc in range(35):
                ow = 128 if oc < 34 else 32
                o0 = oc * 128
                slot, wst, wfn = b.wload((lambda o0_, ow_: lambda s: dict(
                    out=s[:, :8 * ow_].rearrange("p (dc o) -> p dc o", o=ow_),
                    in_=d["w_in"].ap().rearrange("(dc d) o -> d dc o", d=128)[:, :, o0_:o0_ + ow_]))(o0, ow))
                bank = f"pmm{oc % 2}"
                pout = ps[bank][:ow, :]
                for dc in range(DC):
                    wt = [wst] if dc == 0 else []
                    if dc == 0:
                        wt += self.bank_wait(bank)
                    incs = []
                    if dc == DC - 1:
                        incs = [b.inc("tm", 1)]
                    b.op("tensor", (lambda o, s, m, st, sp: lambda e: e.matmul(o, s, m, start=st, stop=sp))(
                        pout, slot[:, dc * ow:(dc + 1) * ow], RH.sl(dc, t0, t0 + TCH), dc == 0, dc == DC - 1),
                        waits=wt, incs=incs)
                tmv = b.val("tm")
                if oc < 16:
                    b.op("scalar", (lambda o, i: lambda e: e.activation(o, i, AF.Silu, bias=0.0, scale=1.0))(
                        ZS[oc], pout),
                        waits=[("tm", tmv), ("vm", b.val("vm"))], incs=[b.inc("sm", 1), b.inc(wfn, 1)])
                    self.bank_set(bank, "sm")
                elif oc < 34:
                    i = oc - 16
                    rw = raw[i % 2]
                    b.op("vector", (lambda o, i_: lambda e: e.tensor_copy(o, i_))(
                        raw.sl(i % 2, 0, 4), b.sb["carry"][:, 4 * i:4 * i + 4]),
                        waits=[("tm", tmv)], incs=[b.inc("vm", 1), b.inc(wfn, 1)])
                    b.op("vector", (lambda o, i_: lambda e: e.tensor_copy(o, i_))(
                        raw.sl(i % 2, 4, 516), pout),
                        incs=[b.inc("vm", 1)])
                    self.bank_set(bank, "vm")
                    cv = cvt[i % 2]
                    w0 = [("sm", silu_sm[i - 2])] if i >= 2 else []
                    b.op("vector", (lambda o, i_, w1, b1: lambda e: e.tensor_scalar(o, i_, w1, b1, ALU.mult, ALU.add))(
                        cv, raw.sl(i % 2, 1, 513), b.sb["cw"][:, 4 * i:4 * i + 1], b.sb["cb"][:, i:i + 1]),
                        waits=w0, incs=[b.inc("vm", 1)])
                    for k in (1, 2, 3):
                        b.op("vector", (lambda o, i_, w1: lambda e: e.scalar_tensor_tensor(o, i_, w1, o, ALU.mult, ALU.add))(
                            cv, raw.sl(i % 2, 1 + k, 513 + k), b.sb["cw"][:, 4 * i + k:4 * i + k + 1]),
                            incs=[b.inc("vm", 1)])
                    b.op("vector", (lambda o, i_: lambda e: e.tensor_copy(o, i_))(
                        b.sb["carry"][:, 4 * i:4 * i + 4], raw.sl(i % 2, 512, 516)),
                        incs=[b.inc("vm", 1)])
                    b.op("scalar", (lambda o, i_: lambda e: e.activation(o, i_, AF.Silu, bias=0.0, scale=1.0))(
                        XBCS[i], cv),
                        waits=[("vm", b.val("vm"))], incs=[b.inc("sm", 1)])
                    silu_sm[i] = b.val("sm")
                else:
                    b.op("vector", (lambda o, i_: lambda e: e.tensor_copy(o, i_))(dtr, pout[0:32, :]),
                         waits=[("tm", tmv)], incs=[b.inc("vm", 1), b.inc(wfn, 1)])
                    self.bank_set(bank, "vm")
            b.op("vector", lambda e: e.tensor_scalar(t32a, dtr, b.sb["dtb"][0:32, :], None, ALU.add),
                 incs=[b.inc("vm", 1)])
            vmv = b.val("vm")
            b.op("scalar", lambda e: e.activation(t32b, t32a, AF.Abs, bias=0.0, scale=1.0),
                 waits=[("vm", vmv)], incs=[b.inc("sm", 1)])
            b.op("scalar", lambda e: e.activation(t32b, t32b, AF.Exp, bias=0.0, scale=-1.0),
                 incs=[b.inc("sm", 1)])
            b.op("scalar", lambda e: e.activation(t32b, t32b, AF.Ln, bias=1.0, scale=1.0),
                 incs=[b.inc("sm", 1)])
            smv = b.val("sm")
            b.op("vector", lambda e: e.tensor_scalar(dtsp, t32a, 0.0, None, ALU.max),
                 incs=[b.inc("vm", 1)])
            b.op("vector", lambda e: e.tensor_tensor(dtsp, dtsp, t32b, ALU.add),
                 waits=[("sm", smv)], incs=[b.inc("vm", 1)])
            b.op("vector", lambda e: e.tensor_scalar(nlog, dtsp, b.sb["expA"][0:32, :], None, ALU.mult),
                 waits=[("sm", self.b.val("sm"))], incs=[b.inc("vm", 1)])
            if self.dev_stop == "zx":
                self.devdump_tiles(XBCS, 18, 0, t0)
                self.devdump_tiles(ZS, 16, 2560, t0)
                b.op("gpsimd", (lambda a, t: lambda e: e.dma_start(out=a, in_=t))(
                    self.d["o_dev"].ap()[2304:2336, t0:t0 + TCH], dtsp),
                    waits=[("vm", b.val("vm"))], incs=[b.inc("xg", 16)])
                self.devdump_barrier()
            for sc in range(4):
                g = tc * 4 + sc
                s0 = sc * SUB
                b.op("vector", (lambda o, d0: lambda e: e.tensor_tensor_scan(o, d0, b.sb["z32"][:], 0.0, ALU.add, ALU.add))(
                    Cc, b.AF[0:32, 4112 + s0:4112 + s0 + SUB]),
                    waits=[("xs", b.val("xs")), ("tm", b.val("tm"))], incs=[b.inc("vm", 1)])
                vmv = b.val("vm")
                b.op("tensor", lambda e: e.transpose(ps["pst"][:, 0:32], Cc, b.sb["identf"][0:32, 0:32]),
                     waits=[("vm", vmv)] + self.bank_wait("pst"), incs=[b.inc("tm", 1)])
                b.op("tensor", (lambda d0: lambda e: e.transpose(ps["pst"][:, 32:64], d0, b.sb["identf"][0:32, 0:32]))(
                    b.AF[0:32, 3600 + s0:3600 + s0 + SUB]),
                    incs=[b.inc("tm", 1)])
                b.op("vector", lambda e: e.tensor_copy(CtDt, ps["pst"][:, 0:64]),
                     waits=[("tm", b.val("tm"))], incs=[b.inc("vm", 1)])
                self.bank_set("pst", "vm")
                b.op("sync", lambda e: e.dma_start(out=Cflat, in_=Cc),
                     waits=[("vm", b.val("vm"))], incs=[b.inc("xs", 16)])
                xsv = b.val("xs")
                bsl = XBCS.sl(16, s0, s0 + SUB)
                csl2 = XBCS.sl(17, s0, s0 + SUB)
                b.op("tensor", (lambda s, m: lambda e: e.matmul(ps["pmm2"][:, 0:128], s, m, start=True, stop=True))(bsl, csl2),
                     waits=self.bank_wait("pmm2") + [("sm", b.val("sm"))], incs=[b.inc("tm", 1)])
                gtm = b.val("tm")
                b.op("tensor", (lambda s: lambda e: e.transpose(ps["ptr"][:, 0:128], s, b.sb["identb"][:]))(bsl),
                     waits=self.bank_wait("ptr"), incs=[b.inc("tm", 1)])
                b.op("vector", lambda e: e.tensor_copy(BTb, ps["ptr"][:, 0:128]),
                     waits=[("tm", b.val("tm"))], incs=[b.inc("vm", 1)])
                self.bank_set("ptr", "vm")
                for h in range(NH_l):
                    hp = h // 2
                    hh = h % 2
                    b.op("tensor", (lambda mv: lambda e: e.matmul(ps["pmm3"][:, 0:128], b.sb["ones1"][0:1, :], mv, start=True, stop=True))(
                        b.AF[0:1, 4624 + h * 128:4624 + (h + 1) * 128]),
                        waits=[("xs", xsv)] + self.bank_wait("pmm3"), incs=[b.inc("tm", 1)])
                    b.op("vector", lambda e: e.tensor_tensor(e1s, ps["pmm3"][:, 0:128], b.sb["upS"][:], ALU.add),
                         waits=[("tm", b.val("tm")), ("sm", b.val("sm"))], incs=[b.inc("vm", 1)])
                    self.bank_set("pmm3", "vm")
                    b.op("scalar", (lambda bc: lambda e: e.activation(dcy, e1s, AF.Exp, bias=bc, scale=-1.0))(
                        CtDt[:, h:h + 1]),
                        waits=[("vm", b.val("vm"))], incs=[b.inc("sm", 1)])
                    b.op("vector", lambda e: e.tensor_tensor(Mp, ps["pmm2"][:, 0:128], dcy, ALU.mult),
                         waits=[("sm", b.val("sm")), ("tm", b.val("tm"))], incs=[b.inc("vm", 1)])
                    if hh == 0:
                        b.op("tensor", (lambda s: lambda e: e.transpose(ps["ptr"][:, 128:256], s, b.sb["identb"][:]))(
                            XBCS.sl(hp, s0, s0 + SUB)),
                            waits=self.bank_wait("ptr"), incs=[b.inc("tm", 1)])
                        b.op("vector", lambda e: e.tensor_copy(XTb, ps["ptr"][:, 128:256]),
                             waits=[("tm", b.val("tm"))], incs=[b.inc("vm", 1)])
                        self.bank_set("ptr", "vm")
                    b.op("vector", (lambda xsl, sc_: lambda e: e.tensor_scalar(XTs, xsl, sc_, None, ALU.mult))(
                        XTb[:, 64 * hh:64 * hh + 64], CtDt[:, 32 + h:33 + h]),
                        waits=[("tm", b.val("tm"))], incs=[b.inc("vm", 1)])
                    if self.dev_stop == "ssd0" and g == 0 and h == 0:
                        b.op("gpsimd", (lambda a: lambda e: e.dma_start(out=a, in_=XTs))(
                            self.d["o_dev"].ap()[3000:3128, 0:64]),
                            waits=[("vm", b.val("vm"))], incs=[b.inc("xg", 16)])
                        xv9 = b.val("xg")
                        b.op("vector", lambda e: None, waits=[("xg", xv9)], incs=[b.inc("vm", 1)])
                    yb = f"py{h % 2}"
                    vmv2 = b.val("vm")
                    b.op("tensor", (lambda o, s, m, sp: lambda e: e.matmul(o, s, m, start=True, stop=sp))(
                        ps[yb][0:64, 0:128], XTs, Mp, g == 0),
                        waits=[("vm", vmv2)] + self.bank_wait(yb), incs=[b.inc("tm", 1)] if g == 0 else [])
                    if g > 0:
                        b.op("scalar", (lambda iv: lambda e: e.activation(b.sb["erow"][:], iv, AF.Exp, bias=0.0, scale=-1.0))(
                            b.AF[0:1, 4624 + h * 128:4624 + (h + 1) * 128]),
                            waits=[("tm", b.val("tm"))], incs=[b.inc("sm", 1)])
                        b.op("tensor", lambda e: e.matmul(ps["pmm3"][:, 0:128], b.sb["ones1"][0:1, :], b.sb["erow"][:], start=True, stop=True),
                             waits=[("sm", b.val("sm"))] + self.bank_wait("pmm3"), incs=[b.inc("tm", 1)])
                        b.op("vector", (lambda cm: lambda e: e.tensor_tensor(Cpp, cm, ps["pmm3"][:, 0:128], ALU.mult))(csl2),
                             waits=[("tm", b.val("tm"))], incs=[b.inc("vm", 1)])
                        self.bank_set("pmm3", "vm")
                        b.op("tensor", (lambda o, s: lambda e: e.matmul(o, s, Cpp, start=False, stop=True))(
                            ps[yb][0:64, 0:128], Stile[h]),
                            waits=[("vm", b.val("vm"))], incs=[b.inc("tm", 1)])
                    ytm = b.val("tm")
                    b.op("vector", (lambda sc_: lambda e: e.tensor_scalar(BTs, BTb, sc_, None, ALU.mult))(
                        dcy[:, 127:128]),
                        waits=[("tm", b.val("tm"))], incs=[b.inc("vm", 1)])
                    b.op("tensor", lambda e: e.matmul(ps["pmm0"][:, 0:64], BTs, XTs, start=True, stop=True),
                         waits=[("vm", b.val("vm"))] + self.bank_wait("pmm0"), incs=[b.inc("tm", 1)])
                    if g > 0:
                        b.op("tensor", lambda e: e.matmul(ps["pmm3"][:, 0:1], b.sb["ones1"][0:1, :], b.sb["erow"][:, 127:128], start=True, stop=True),
                             waits=[("sm", b.val("sm"))] + self.bank_wait("pmm3"),
                             incs=[b.inc("tm", 1)])
                        b.op("vector", lambda e: e.tensor_copy(b.sb["esc"][:], ps["pmm3"][:, 0:1]),
                             waits=[("tm", b.val("tm"))], incs=[b.inc("vm", 1)])
                        self.bank_set("pmm3", "vm")
                    if g == 0:
                        b.op("vector", (lambda o, i_: lambda e: e.tensor_copy(o, i_))(Stile[h], ps["pmm0"][:, 0:64]),
                             waits=[("tm", b.val("tm"))], incs=[b.inc("vm", 1)])
                    else:
                        b.op("vector", (lambda o: lambda e: e.scalar_tensor_tensor(o, o, b.sb["esc"][:, 0:1], ps["pmm0"][:, 0:64], ALU.mult, ALU.add))(
                            Stile[h]),
                            waits=[("tm", b.val("tm"))], incs=[b.inc("vm", 1)])
                    self.bank_set("pmm0", "vm")
                    xmsl = b.AB[64 * hh:64 * hh + 64,
                                XBCS.off + hp * 512 + s0:XBCS.off + hp * 512 + s0 + SUB]
                    ydst = b.AB[64 * hh:64 * hh + 64,
                                Y16.off + hp * 512 + s0:Y16.off + hp * 512 + s0 + SUB]
                    b.op("vector", (lambda o, i_, sc_, p_: lambda e: e.scalar_tensor_tensor(o, i_, sc_, p_, ALU.mult, ALU.add))(
                        ydst, xmsl, b.sb["Drep"][64 * hh:64 * hh + 64, hp:hp + 1], ps[yb][0:64, 0:128]),
                        waits=[("tm", ytm)], incs=[b.inc("vm", 1)])
                    self.bank_set(yb, "vm")
                    if self.dev_stop == "ssd0" and g == 0 and h == 0:
                        dd = self.d["o_dev"].ap()
                        vmv9 = b.val("vm")
                        for (r0, w9, t9, eng9) in [(0, 128, dcy, "sync"), (128, 128, e1s, "sync"),
                                                   (256, 64, CtDt, "sync"), (384, 128, Mp, "gpsimd"),
                                                   (512, 128, XTb, "gpsimd"), (640, 64, XTs, "gpsimd"),
                                                   (1040, 512, dtsp, "gpsimd"), (1104 + 512 - 512, 0, None, "none")]:
                            if t9 is None:
                                continue
                            nr = 32 if t9 is dtsp else 128
                            b.op(eng9, (lambda a, t: lambda e: e.dma_start(out=a, in_=t))(
                                dd[r0:r0 + nr, 0:w9], t9),
                                waits=[("vm", vmv9), ("sm", b.val("sm"))],
                                incs=[b.inc("xs" if eng9 == "sync" else "xg", 16)])
                        b.op("gpsimd", (lambda a, t: lambda e: e.dma_start(out=a, in_=t))(
                            dd[704:736, 0:128], Cc), waits=[("vm", vmv9)], incs=[b.inc("xg", 16)])
                        b.op("vector", lambda e: e.tensor_copy(sqv[:, 0:128], ps["pmm2"][:, 0:128]),
                             incs=[b.inc("vm", 1)])
                        b.op("sync", (lambda a: lambda e: e.dma_start(out=a, in_=sqv[:, 0:128]))(
                            dd[736:864, 0:128]), waits=[("vm", b.val("vm"))], incs=[b.inc("xs", 16)])
                        b.op("gpsimd", (lambda a: lambda e: e.dma_start(out=a, in_=Y16.sl(0, 0, 128)))(
                            dd[896:1024, 0:128]), waits=[("vm", b.val("vm"))], incs=[b.inc("xg", 16)])
                        self.devdump_barrier()
                self.bank_set("pmm2", "vm")
            if self.dev_stop == "ssdy":
                self.devdump_tiles(Y16, 16, 0, t0)
                self.devdump_barrier()
            for i in range(16):
                b.op("vector", (lambda o, z_: lambda e: e.tensor_tensor(o, o, z_, ALU.mult))(Y16[i], ZS[i]),
                     waits=[("tm", b.val("tm"))] + ([("sm", b.val("sm"))] if i == 0 else []),
                     incs=[b.inc("vm", 1)])
                b.op("vector", (lambda o, i_: lambda e: e.tensor_tensor(o, i_, i_, ALU.mult))(sqv, Y16[i]),
                     incs=[b.inc("vm", 1)])
                b.op("tensor", (lambda st, sp: lambda e: e.matmul(ps["pst"][0:1, :], b.sb["onesf"][:], sqv, start=st, stop=sp))(
                    i == 0, i == 15),
                    waits=[("vm", b.val("vm"))] + (self.bank_wait("pst") if i == 0 else []),
                    incs=[b.inc("tm", 1)])
            tmv = b.val("tm")
            tr0 = b.sb["R4b"][0:1, 0:512]
            tr1 = b.sb["R4b"][0:1, 512:1024]
            b.op("vector", lambda e: e.tensor_scalar(tr0, ps["pst"][0:1, :], 1.0 / DIN, None, ALU.mult),
                 waits=[("tm", tmv)], incs=[b.inc("vm", 1)])
            self.bank_set("pst", "vm")
            b.op("scalar", lambda e: e.activation(tr1, tr0, AF.Ln, bias=b.sb["eps5"][:], scale=1.0),
                 waits=[("vm", b.val("vm"))], incs=[b.inc("sm", 1)])
            b.op("scalar", lambda e: e.activation(tr0, tr1, AF.Exp, bias=0.0, scale=-0.5),
                 incs=[b.inc("sm", 1)])
            b.op("tensor", lambda e: e.matmul(ps["pst"][:], b.sb["ones1"][0:1, :], tr0, start=True, stop=True),
                 waits=[("sm", b.val("sm"))] + self.bank_wait("pst"), incs=[b.inc("tm", 1)])
            rtm = b.val("tm")
            for i in range(16):
                b.op("vector", (lambda o, i_, sc_: lambda e: e.scalar_tensor_tensor(o, i_, sc_, ps["pst"][:], ALU.mult, ALU.mult))(
                    YB[i], Y16[i], b.sb["mnw"][:, i:i + 1]),
                    waits=[("tm", rtm)] if i == 0 else [], incs=[b.inc("vm", 1)])
            self.bank_set("pst", "vm")
            if self.dev_stop == "ymul":
                self.devdump_tiles(YB, 16, 0, t0)
                b.op("vector", (lambda o: lambda e: e.tensor_copy(o, ps["pst"][0:1, 0:512]))(
                    b.sb["R4b"][0:1, 1024:1536]), incs=[b.inc("vm", 1)])
                b.op("gpsimd", (lambda a, t: lambda e: e.dma_start(out=a, in_=t))(
                    self.d["o_dev"].ap()[2304:2305, t0:t0 + TCH], b.sb["R4b"][0:1, 1024:1536]),
                    waits=[("vm", b.val("vm"))], incs=[b.inc("xg", 16)])
                self.devdump_barrier()
            for oc in range(8):
                slot, wst, wfn = b.wload((lambda oc_: lambda s: dict(
                    out=s[:, :2048].rearrange("p (dc o) -> p dc o", o=128),
                    in_=d["w_out"].ap().rearrange("(dc d) o -> d dc o", d=128)[:, :, oc_ * 128:(oc_ + 1) * 128]))(oc))
                bank = f"pmm{oc % 2}"
                for dc2 in range(16):
                    wt = [wst] if dc2 == 0 else []
                    if dc2 == 0:
                        wt += self.bank_wait(bank)
                    incs = []
                    if dc2 == 15:
                        incs = [b.inc("tm", 1)]
                    b.op("tensor", (lambda o, s, m, st, sp: lambda e: e.matmul(o, s, m, start=st, stop=sp))(
                        ps[bank][:], slot[:, dc2 * 128:(dc2 + 1) * 128], YB[dc2], dc2 == 0, dc2 == 15),
                        waits=wt + ([("vm", b.val("vm"))] if dc2 == 0 else []), incs=incs)
                tmv = b.val("tm")
                ta = raw[oc % 2]
                rsl = slice(oc * 128, (oc + 1) * 128)
                b.op("sync", (lambda a, t: lambda e: e.dma_start(out=t[:, 0:512], in_=a))(
                    u_d[rsl, csl], ta),
                    waits=[("vm", b.val("vm"))], incs=[b.inc("xs", 16)])
                b.op("vector", (lambda t, p_: lambda e: e.tensor_tensor(t[:, 0:512], t[:, 0:512], p_, ALU.add))(ta, ps[bank][:]),
                     waits=[("xs", b.val("xs")), ("tm", tmv)], incs=[b.inc("vm", 1), b.inc(wfn, 1)])
                self.bank_set(bank, "vm")
                if self.dev_stop == "wout0" and tc == 0:
                    dd = self.d["o_dev"].ap()
                    b.op("vector", (lambda p_: lambda e: e.tensor_copy(b.AF[:, 1040:1552], p_))(
                        ps[bank][:]), waits=[("tm", tmv)], incs=[b.inc("vm", 1)])
                    self.bank_set(bank, "vm")
                    b.op("sync", (lambda a: lambda e: e.dma_start(out=a, in_=b.AF[:, 1040:1552]))(
                        dd[oc * 128:(oc + 1) * 128, 512:1024]),
                        waits=[("vm", b.val("vm"))], incs=[b.inc("xs", 16)])
                    b.op("sync", (lambda a, t: lambda e: e.dma_start(out=a, in_=t[:, 0:512]))(
                        dd[oc * 128:(oc + 1) * 128, 0:512], ta),
                        waits=[("vm", b.val("vm"))], incs=[b.inc("xs", 16)])
                    b.op("gpsimd", (lambda a, s: lambda e: e.dma_start(out=a, in_=s[:, 0:2048]))(
                        dd[1024 + oc * 128:1152 + oc * 128, 0:2048], slot),
                        waits=[("vm", b.val("vm"))], incs=[b.inc("xg", 16)])
                    self.devdump_barrier()
                b.op("sync", (lambda a, t: lambda e: e.dma_start(out=a, in_=t[:, 0:512]))(
                    u_d[rsl, csl], ta),
                    waits=[("vm", b.val("vm"))], incs=[b.inc("xs", 16)])
            b.op("vector", lambda e: None, waits=[("xs", b.val("xs"))], incs=[b.inc("vm", 1)])

    def emit_mqa(self, RH, u_d):
        b, d = self.b, self.d
        ps = b.ps
        PH = self.PH
        QP = Tiles(b.AB, PH + 16384, 8, 2048)
        KVo = PH + 32768
        KV = b.AB[:, KVo:KVo + 2048]
        KD = b.AB[:, KVo + 2048:KVo + 4096]
        VT = Tiles(b.AB, KVo + 4096, 16, 65)
        PBt = Tiles(b.AB, KVo + 4096 + 1040, 2, 512)
        YH = Tiles(b.AB, KVo + 6160, 16, 512)
        YP = Tiles(b.AB, KVo + 6160 + 8192, 8, 512)
        raw = self.AFt(0, 2, 520)
        cvt = self.AFt(1040, 2, 512)
        yf = b.AF[0:64, 2064:2576]
        b.op("vector", lambda e: e.memset(b.sb["carry"][:, 0:18], 0.0), incs=[b.inc("vm", 1)])
        for tc in range(NTCH):
            t0 = tc * TCH
            for oc in range(9):
                slot, wst, wfn = b.wload((lambda oc_: lambda s: dict(
                    out=s[:, :1024].rearrange("p (dc o) -> p dc o", o=128),
                    in_=d["w_qkv"].ap().rearrange("(dc d) o -> d dc o", d=128)[:, :, oc_ * 128:(oc_ + 1) * 128]))(oc))
                bank = f"pmm{oc % 2}"
                pout = ps[bank][:]
                for dc in range(DC):
                    wt = ([wst] + self.bank_wait(bank)) if dc == 0 else []
                    b.op("tensor", (lambda o, s_, m, st, sp: lambda e: e.matmul(o, s_, m, start=st, stop=sp))(
                        pout, slot[:, dc * 128:(dc + 1) * 128], RH.sl(dc, t0, t0 + TCH), dc == 0, dc == DC - 1),
                        waits=wt, incs=[b.inc("tm", 1)] if dc == DC - 1 else [])
                tmv = b.val("tm")
                r2 = oc % 2
                b.op("vector", (lambda o, i_: lambda e: e.tensor_copy(o, i_))(
                    raw.sl(r2, 0, 2), b.sb["carry"][:, 2 * oc:2 * oc + 2]),
                    waits=[("tm", tmv)], incs=[b.inc("vm", 1), b.inc(wfn, 1)])
                b.op("vector", (lambda o, i_: lambda e: e.tensor_copy(o, i_))(
                    raw.sl(r2, 2, 514), pout), incs=[b.inc("vm", 1)])
                self.bank_set(bank, "vm")
                wnm, bnm = ("qcw", "qcb") if oc < 8 else ("kvw", "kvb")
                cv = cvt[r2]
                b.op("vector", (lambda o, i_, w1, b1: lambda e: e.tensor_scalar(o, i_, w1, b1, ALU.mult, ALU.add))(
                    cv, raw.sl(r2, 0, 512), b.sb[wnm][:, 0:1], b.sb[bnm][:, 0:1]),
                    incs=[b.inc("vm", 1)])
                b.op("vector", (lambda o, i_, w1: lambda e: e.scalar_tensor_tensor(o, i_, w1, o, ALU.mult, ALU.add))(
                    cv, raw.sl(r2, 1, 513), b.sb[wnm][:, 1:2]), incs=[b.inc("vm", 1)])
                dst = QP[oc][:, t0:t0 + TCH] if oc < 8 else KV[:, t0:t0 + TCH]
                b.op("vector", (lambda o, i_, w1, c_: lambda e: e.scalar_tensor_tensor(o, i_, w1, c_, ALU.mult, ALU.add))(
                    dst, raw.sl(r2, 2, 514), b.sb[wnm][:, 2:3], cv), incs=[b.inc("vm", 1)])
                b.op("vector", (lambda o, i_: lambda e: e.tensor_copy(o, i_))(
                    b.sb["carry"][:, 2 * oc:2 * oc + 2], raw.sl(r2, 512, 514)), incs=[b.inc("vm", 1)])
        vq = b.val("vm")
        b.op("sync", lambda e: e.dma_start(out=KD[64:128, :], in_=KV[0:64, :]),
             waits=[("vm", vq)], incs=[b.inc("xs", 16)])
        for tt in range(16):
            b.op("tensor", (lambda i_: lambda e: e.transpose(ps["ptr"][:, 0:64], i_, b.sb["identb"][64:128, 64:128]))(
                KV[64:128, 128 * tt:128 * tt + 128]),
                waits=([("vm", vq)] if tt == 0 else []) + self.bank_wait("ptr"), incs=[b.inc("tm", 1)])
            b.op("vector", (lambda o: lambda e: e.tensor_copy(o, ps["ptr"][:, 0:64]))(VT[tt][:, 0:64]),
                 waits=[("tm", b.val("tm"))], incs=[b.inc("vm", 1)])
            self.bank_set("ptr", "vm")
            b.op("vector", (lambda o: lambda e: e.tensor_copy(o, b.sb["onesf"][:]))(VT[tt][:, 64:65]),
                 incs=[b.inc("vm", 1)])
        if self.dev_stop == "qkv":
            self.devdump_tiles(QP, 8, 0, 0, w=2048)
            b.op("gpsimd", (lambda a: lambda e: e.dma_start(out=a, in_=KV))(
                self.d["o_dev"].ap()[1024:1152, 0:2048]),
                waits=[("vm", b.val("vm")), ("xs", b.val("xs"))], incs=[b.inc("xg", 16)])
            self.devdump_barrier()
            return
        for s in range(NTCH):
            tq0 = s * TCH
            csl = slice(tq0, tq0 + TCH)
            if s > 0:
                b.op("vector", lambda e: None, waits=[("xs", b.val("xs"))], incs=[b.inc("vm", 1)])
            for h in range(NAH):
                hp, hh = h // 2, h % 2
                ntt = 4 * s + 4
                qmov = QP[hp][64 * hh:64 * hh + 64, tq0:tq0 + TCH]
                sm_exp = {}

                def emit_av(t, ntt=ntt):
                    wt = [("sm", sm_exp[t])]
                    if t == 0:
                        wt += self.bank_wait("pmm2")
                    b.op("tensor", (lambda st_, m, ss, pp: lambda e: e.matmul(ps["pmm2"][0:65, :], st_, m, start=ss, stop=pp))(
                        VT[t], PBt[t % 2], t == 0, t == ntt - 1),
                        waits=wt, incs=[b.inc("tm", 1)])
                for tt in range(ntt):
                    pb = f"py{tt % 2}"
                    stat = KV[0:64, 128 * tt:128 * tt + 128] if hh == 0 else KD[64:128, 128 * tt:128 * tt + 128]
                    diag = tt >= 4 * s
                    wt0 = self.bank_wait(pb)
                    if s == 0 and h == 0 and tt == 0:
                        wt0 += [("xs", b.val("xs")), ("vm", b.val("vm")), ("sm", b.val("sm"))]
                    b.op("tensor", (lambda o, s_, m, sp: lambda e: e.matmul(o, s_, m, start=True, stop=sp))(
                        ps[pb][:], stat, qmov, not diag),
                        waits=wt0, incs=[b.inc("tm", 1)] if not diag else [])
                    if diag:
                        b.op("tensor", (lambda o, m: lambda e: e.matmul(o, b.sb["identb"][:], m, start=False, stop=True))(
                            ps[pb][:], b.sb["maskB"][:, 512 * (tt - 4 * s):512 * (tt - 4 * s) + 512]),
                            incs=[b.inc("tm", 1)])
                    b.op("scalar", (lambda o, p_: lambda e: e.activation(o, p_, AF.Exp, bias=0.0, scale=0.125))(
                        PBt[tt % 2], ps[pb][:]),
                        waits=[("tm", b.val("tm"))], incs=[b.inc("sm", 1)])
                    sm_exp[tt] = b.val("sm")
                    self.bank_set(pb, "sm")
                    if tt >= 1:
                        emit_av(tt - 1)
                emit_av(ntt - 1)
                tm_av = b.val("tm")
                rrow = b.sb["R4b"][64:65, 0:512]
                b.op("vector", (lambda o: lambda e: e.reciprocal(o, ps["pmm2"][64:65, :]))(rrow),
                     waits=[("tm", tm_av)], incs=[b.inc("vm", 1)])
                b.op("vector", (lambda o: lambda e: e.tensor_copy(o, ps["pmm2"][0:64, :]))(yf),
                     incs=[b.inc("vm", 1)])
                self.bank_set("pmm2", "vm")
                b.op("tensor", lambda e: e.matmul(ps["pmm3"][0:64, :], b.sb["ones1"][64:65, 0:64], rrow, start=True, stop=True),
                     waits=[("vm", b.val("vm"))] + self.bank_wait("pmm3"), incs=[b.inc("tm", 1)])
                b.op("vector", (lambda o: lambda e: e.tensor_tensor(o, yf, ps["pmm3"][0:64, :], ALU.mult))(
                    YH[h][0:64, :]),
                    waits=[("tm", b.val("tm"))], incs=[b.inc("vm", 1)])
                self.bank_set("pmm3", "vm")
            vv = b.val("vm")
            b.op("sync", lambda e: e.dma_start(
                out=b.AB[0:64, YP.off:YP.off + 4096].rearrange("p (j c) -> p j c", c=512),
                in_=b.AB[0:64, YH.off:YH.off + 8192].rearrange("p (j c) -> p j c", c=1024)[:, :, 0:512]),
                waits=[("vm", vv)], incs=[b.inc("xs", 16)])
            b.op("sync", lambda e: e.dma_start(
                out=b.AB[64:128, YP.off:YP.off + 4096].rearrange("p (j c) -> p j c", c=512),
                in_=b.AB[0:64, YH.off:YH.off + 8192].rearrange("p (j c) -> p j c", c=1024)[:, :, 512:1024]),
                waits=[("xs", b.val("xs"))], incs=[b.inc("xs", 16)])
            xsv = b.val("xs")
            for oc in range(8):
                slot, wst, wfn = b.wload((lambda oc_: lambda s: dict(
                    out=s[:, :1024].rearrange("p (dc o) -> p dc o", o=128),
                    in_=d["w_cproj"].ap().rearrange("(dc d) o -> d dc o", d=128)[:, :, oc_ * 128:(oc_ + 1) * 128]))(oc))
                bank = f"pmm{oc % 2}"
                for dc2 in range(DC):
                    wt = ([wst] + self.bank_wait(bank) + [("xs", xsv)]) if dc2 == 0 else []
                    b.op("tensor", (lambda o, s_, m, st, sp: lambda e: e.matmul(o, s_, m, start=st, stop=sp))(
                        ps[bank][:], slot[:, dc2 * 128:(dc2 + 1) * 128], YP[dc2], dc2 == 0, dc2 == DC - 1),
                        waits=wt, incs=[b.inc("tm", 1)] if dc2 == DC - 1 else [])
                tmv = b.val("tm")
                ta = raw[oc % 2]
                rsl = slice(oc * 128, (oc + 1) * 128)
                b.op("sync", (lambda a, t: lambda e: e.dma_start(out=t[:, 0:512], in_=a))(
                    u_d[rsl, csl], ta),
                    waits=[("vm", b.val("vm"))], incs=[b.inc("xs", 16)])
                b.op("vector", (lambda t, p_: lambda e: e.tensor_tensor(t[:, 0:512], t[:, 0:512], p_, ALU.add))(ta, ps[bank][:]),
                     waits=[("xs", b.val("xs")), ("tm", tmv)], incs=[b.inc("vm", 1), b.inc(wfn, 1)])
                self.bank_set(bank, "vm")
                b.op("sync", (lambda a, t: lambda e: e.dma_start(out=a, in_=t[:, 0:512]))(
                    u_d[rsl, csl], ta),
                    waits=[("vm", b.val("vm"))], incs=[b.inc("xs", 16)])
        b.op("vector", lambda e: None, waits=[("xs", b.val("xs"))], incs=[b.inc("vm", 1)])

    def emit_rwkv(self, RH, u_d):
        b, d = self.b, self.d
        ps = b.ps
        PH = self.PH
        XK = Tiles(b.AB, PH + 16384, 8, 2048)
        XR = Tiles(b.AB, PH + 32768, 8, 2048)
        KF = Tiles(b.AB, PH, 32, 512)
        raw = self.AFt(0, 2, 520)
        SH = b.AF[:, 2064:4112]
        kffp = b.AF[:, 2064:2576]
        SR = b.AF[:, 1040:1552]
        t1 = b.AF[:, 1552:2064]
        for dc in range(DC):
            b.op("vector", (lambda o, i_: lambda e: e.tensor_scalar(o, i_, -1.0, None, ALU.mult))(
                SH[:, 0:1], RH.sl(dc, 0, 1)), incs=[b.inc("vm", 1)])
            b.op("vector", (lambda o, a_, b_: lambda e: e.tensor_tensor(o, a_, b_, ALU.subtract))(
                SH[:, 1:2048], RH.sl(dc, 0, 2047), RH.sl(dc, 1, 2048)), incs=[b.inc("vm", 1)])
            b.op("vector", (lambda o, sc_, h_: lambda e: e.scalar_tensor_tensor(o, SH, sc_, h_, ALU.mult, ALU.add))(
                XK[dc], b.sb["maa_k"][:, dc:dc + 1], RH[dc]), incs=[b.inc("vm", 1)])
            b.op("vector", (lambda o, sc_, h_: lambda e: e.scalar_tensor_tensor(o, SH, sc_, h_, ALU.mult, ALU.add))(
                XR[dc], b.sb["maa_r"][:, dc:dc + 1], RH[dc]), incs=[b.inc("vm", 1)])
        mixvm = b.val("vm")
        for tc in range(NTCH):
            t0 = tc * TCH
            csl = slice(t0, t0 + TCH)
            for fc in range(FFC):
                slot, wst, wfn = b.wload((lambda fc_: lambda s: dict(
                    out=s[:, :1024].rearrange("p (dc f) -> p dc f", f=128),
                    in_=d["w_key"].ap().rearrange("(dc d) f -> d dc f", d=128)[:, :, fc_ * 128:(fc_ + 1) * 128]))(fc))
                bank = f"pmm{fc % 2}"
                for dcc in range(DC):
                    wt = ([wst] + self.bank_wait(bank)) if dcc == 0 else []
                    if tc == 0 and fc == 0 and dcc == 0:
                        wt += [("vm", mixvm)]
                    b.op("tensor", (lambda o, s_, m, st, sp: lambda e: e.matmul(o, s_, m, start=st, stop=sp))(
                        ps[bank][:], slot[:, dcc * 128:(dcc + 1) * 128], XK.sl(dcc, t0, t0 + TCH), dcc == 0, dcc == DC - 1),
                        waits=wt, incs=[b.inc("tm", 1)] if dcc == DC - 1 else [])
                tmv = b.val("tm")
                b.op("scalar", (lambda p_: lambda e: e.activation(kffp, p_, AF.Relu, bias=0.0, scale=1.0))(
                    ps[bank][:]),
                    waits=[("tm", tmv), ("vm", b.val("vm"))], incs=[b.inc("sm", 1), b.inc(wfn, 1)])
                self.bank_set(bank, "sm")
                b.op("vector", (lambda o: lambda e: e.tensor_tensor(o, kffp, kffp, ALU.mult))(KF[fc]),
                     waits=[("sm", b.val("sm"))], incs=[b.inc("vm", 1)])
            kfvm = b.val("vm")
            for oc in range(8):
                sl1, wst1, wfn1 = b.wload((lambda oc_: lambda s: dict(
                    out=s[:, :2048].rearrange("p (fc o) -> p fc o", o=128),
                    in_=d["w_val"].ap().rearrange("(fc f) o -> f fc o", f=128)[:, 0:16, oc_ * 128:(oc_ + 1) * 128]))(oc))
                sl2, wst2, wfn2 = b.wload((lambda oc_: lambda s: dict(
                    out=s[:, :2048].rearrange("p (fc o) -> p fc o", o=128),
                    in_=d["w_val"].ap().rearrange("(fc f) o -> f fc o", f=128)[:, 16:32, oc_ * 128:(oc_ + 1) * 128]))(oc))
                bank = f"pmm{oc % 2}"
                for fc in range(FFC):
                    sl = sl1 if fc < 16 else sl2
                    wt = []
                    if fc == 0:
                        wt = [wst1] + self.bank_wait(bank) + [("vm", kfvm)]
                    elif fc == 16:
                        wt = [wst2]
                    b.op("tensor", (lambda o, s_, m, st, sp: lambda e: e.matmul(o, s_, m, start=st, stop=sp))(
                        ps[bank][:], sl[:, (fc % 16) * 128:(fc % 16 + 1) * 128], KF[fc], fc == 0, fc == FFC - 1),
                        waits=wt, incs=[b.inc("tm", 1)] if fc == FFC - 1 else [])
                tmv = b.val("tm")
                slr, wstr, wfnr = b.wload((lambda oc_: lambda s: dict(
                    out=s[:, :1024].rearrange("p (dc o) -> p dc o", o=128),
                    in_=d["w_rec"].ap().rearrange("(dc d) o -> d dc o", d=128)[:, :, oc_ * 128:(oc_ + 1) * 128]))(oc))
                bankr = f"pmm{2 + oc % 2}"
                for dcc in range(DC):
                    wt = ([wstr] + self.bank_wait(bankr)) if dcc == 0 else []
                    b.op("tensor", (lambda o, s_, m, st, sp: lambda e: e.matmul(o, s_, m, start=st, stop=sp))(
                        ps[bankr][:], slr[:, dcc * 128:(dcc + 1) * 128], XR.sl(dcc, t0, t0 + TCH), dcc == 0, dcc == DC - 1),
                        waits=wt, incs=[b.inc("tm", 1)] if dcc == DC - 1 else [])
                tmr = b.val("tm")
                b.op("scalar", (lambda p_: lambda e: e.activation(SR, p_, AF.Sigmoid, bias=0.0, scale=1.0))(
                    ps[bankr][:]),
                    waits=[("tm", tmr), ("vm", b.val("vm"))], incs=[b.inc("sm", 1), b.inc(wfnr, 1)])
                self.bank_set(bankr, "sm")
                b.op("vector", (lambda p_: lambda e: e.tensor_tensor(t1, SR, p_, ALU.mult))(ps[bank][:]),
                     waits=[("sm", b.val("sm")), ("tm", tmv)],
                     incs=[b.inc("vm", 1), b.inc(wfn1, 1), b.inc(wfn2, 1)])
                self.bank_set(bank, "vm")
                ta = raw[oc % 2]
                rsl = slice(oc * 128, (oc + 1) * 128)
                b.op("sync", (lambda a, t: lambda e: e.dma_start(out=t[:, 0:512], in_=a))(
                    u_d[rsl, csl], ta),
                    waits=[("vm", b.val("vm"))], incs=[b.inc("xs", 16)])
                b.op("vector", (lambda t: lambda e: e.tensor_tensor(t[:, 0:512], t[:, 0:512], t1, ALU.add))(ta),
                     waits=[("xs", b.val("xs"))], incs=[b.inc("vm", 1)])
                b.op("sync", (lambda a, t: lambda e: e.dma_start(out=a, in_=t[:, 0:512]))(
                    u_d[rsl, csl], ta),
                    waits=[("vm", b.val("vm"))], incs=[b.inc("xs", 16)])
        b.op("vector", lambda e: None, waits=[("xs", b.val("xs"))], incs=[b.inc("vm", 1)])

    def emit_fattn(self, uf_d, ao_d):
        b, d = self.b, self.d
        ps = b.ps
        PH = self.PH
        K8 = Tiles(b.AB, PH, 8, 2048)
        VTP = Tiles(b.AB, PH + 16384, 128, 130)
        UFS = Tiles(b.AB, PH + 33024, 8, 512)
        QS = Tiles(b.AB, PH + 37120, 8, 512)
        PBt = Tiles(b.AB, PH + 41216, 2, 512)
        YHo = Tiles(b.AB, PH + 42240, 8, 512)
        YP = Tiles(b.AB, PH + 46336, 8, 512)
        VP = b.AB[:, PH + 50432:PH + 50944]
        raw = self.AFt(0, 2, 520)
        TKB = b.AF[:, 1040:1552]
        TR1 = b.AF[:, 1552:2064]
        yf = b.AF[0:64, 2064:2576]

        def load_ufs(csl_):
            b.op("gpsimd", (lambda c_: lambda e: e.dma_start(
                out=b.AB[:, UFS.off:UFS.off + 4096].rearrange("p (dc t) -> p dc t", t=512),
                in_=uf_d.rearrange("(dc d) t -> d dc t", d=128)[:, :, c_]))(csl_),
                waits=[("tm", b.val("tm"))], incs=[b.inc("xg", 16)])
            b.op("vector", lambda e: None, waits=[("xg", b.val("xg")), ("xs", b.val("xs"))], incs=[b.inc("vm", 1)])

        def rope_evac(oc, bank, bcol, dst):
            b.op("vector", (lambda p_, bc: lambda e: e.tensor_scalar(TKB, p_, bc, None, ALU.add))(
                ps[bank][:], b.sb[bcol][:, oc:oc + 1]),
                waits=[("tm", b.val("tm"))], incs=[b.inc("vm", 1)])
            self.bank_set(bank, "vm")
            b.op("tensor", lambda e: e.matmul(ps["pst"][:], b.sb["rotP"][:], TKB, start=True, stop=True),
                 waits=[("vm", b.val("vm"))] + self.bank_wait("pst"), incs=[b.inc("tm", 1)])
            return b.val("tm")

        def rope_mix(csl_, dst):
            b.op("vector", (lambda c_: lambda e: e.tensor_tensor(TR1, TKB, b.sb["ropeC"][:, c_], ALU.mult))(csl_),
                 waits=[("tm", b.val("tm"))], incs=[b.inc("vm", 1)])
            b.op("vector", (lambda c_: lambda e: e.tensor_tensor(TKB, ps["pst"][:], b.sb["ropeS"][:, c_], ALU.mult))(csl_),
                 incs=[b.inc("vm", 1)])
            self.bank_set("pst", "vm")
            b.op("vector", (lambda o: lambda e: e.tensor_tensor(o, TR1, TKB, ALU.add))(dst),
                 incs=[b.inc("vm", 1)])

        b.op("vector", lambda e: e.memset(
            b.AB[:, VTP.off:VTP.off + 16640].rearrange("p (t c) -> p t c", c=130)[:, :, 64:65], 1.0),
            incs=[b.inc("vm", 1)])
        b.op("vector", lambda e: e.memset(
            b.AB[:, VTP.off:VTP.off + 16640].rearrange("p (t c) -> p t c", c=130)[:, :, 129:130], 1.0),
            incs=[b.inc("vm", 1)])
        for tc in range(NTCH):
            t0 = tc * TCH
            csl = slice(t0, t0 + TCH)
            load_ufs(csl)
            for oc in range(8):
                slot, wst, wfn = b.wload((lambda oc_: lambda s: dict(
                    out=s[:, :1024].rearrange("p (dc o) -> p dc o", o=128),
                    in_=d["Wv"].ap().rearrange("(dc d) o -> d dc o", d=128)[:, :, oc_ * 128:(oc_ + 1) * 128]))(oc))
                bank = f"pmm{oc % 2}"
                for dc in range(DC):
                    wt = ([wst] + self.bank_wait(bank) + [("vm", b.val("vm"))]) if dc == 0 else []
                    b.op("tensor", (lambda o, s_, m, st, sp: lambda e: e.matmul(o, s_, m, start=st, stop=sp))(
                        ps[bank][:], slot[:, dc * 128:(dc + 1) * 128], UFS[dc], dc == 0, dc == DC - 1),
                        waits=wt, incs=[b.inc("tm", 1)] if dc == DC - 1 else [])
                b.op("vector", (lambda p_, bc: lambda e: e.tensor_scalar(VP, p_, bc, None, ALU.add))(
                    ps[bank][:], b.sb["bv"][:, oc:oc + 1]),
                    waits=[("tm", b.val("tm"))], incs=[b.inc("vm", 1), b.inc(wfn, 1)])
                self.bank_set(bank, "vm")
                for j in range(4):
                    b.op("tensor", (lambda j_: lambda e: e.transpose(ps["ptr"][:, 0:128], VP[:, 128 * j_:128 * j_ + 128], b.sb["identb"][:]))(j),
                         waits=self.bank_wait("ptr") + [("vm", b.val("vm"))], incs=[b.inc("tm", 1)])
                    vt = VTP[oc * 16 + 4 * tc + j]
                    b.op("vector", (lambda o: lambda e: e.tensor_copy(o, ps["ptr"][:, 0:64]))(vt[:, 0:64]),
                         waits=[("tm", b.val("tm"))], incs=[b.inc("vm", 1)])
                    b.op("vector", (lambda o: lambda e: e.tensor_copy(o, ps["ptr"][:, 64:128]))(vt[:, 65:129]),
                         incs=[b.inc("vm", 1)])
                    self.bank_set("ptr", "vm")
            for oc in range(8):
                slot, wst, wfn = b.wload((lambda oc_: lambda s: dict(
                    out=s[:, :1024].rearrange("p (dc o) -> p dc o", o=128),
                    in_=d["Wk"].ap().rearrange("(dc d) o -> d dc o", d=128)[:, :, oc_ * 128:(oc_ + 1) * 128]))(oc))
                bank = f"pmm{oc % 2}"
                for dc in range(DC):
                    wt = ([wst] + self.bank_wait(bank)) if dc == 0 else []
                    b.op("tensor", (lambda o, s_, m, st, sp: lambda e: e.matmul(o, s_, m, start=st, stop=sp))(
                        ps[bank][:], slot[:, dc * 128:(dc + 1) * 128], UFS[dc], dc == 0, dc == DC - 1),
                        waits=wt, incs=[b.inc("tm", 1)] if dc == DC - 1 else [])
                rope_evac(oc, bank, "bk", None)
                b.op("vector", lambda e: None, incs=[b.inc(wfn, 1)])
                rope_mix(csl, K8[oc][:, csl])
        for s in range(NTCH):
            tq0 = s * TCH
            csl = slice(tq0, tq0 + TCH)
            load_ufs(csl)
            for oc in range(8):
                slot, wst, wfn = b.wload((lambda oc_: lambda s_: dict(
                    out=s_[:, :1024].rearrange("p (dc o) -> p dc o", o=128),
                    in_=d["Wq"].ap().rearrange("(dc d) o -> d dc o", d=128)[:, :, oc_ * 128:(oc_ + 1) * 128]))(oc))
                bank = f"pmm{oc % 2}"
                for dc in range(DC):
                    wt = ([wst] + self.bank_wait(bank) + [("vm", b.val("vm"))]) if dc == 0 else []
                    b.op("tensor", (lambda o, s_, m, st, sp: lambda e: e.matmul(o, s_, m, start=st, stop=sp))(
                        ps[bank][:], slot[:, dc * 128:(dc + 1) * 128], UFS[dc], dc == 0, dc == DC - 1),
                        waits=wt, incs=[b.inc("tm", 1)] if dc == DC - 1 else [])
                rope_evac(oc, bank, "bq", None)
                b.op("vector", lambda e: None, incs=[b.inc(wfn, 1)])
                rope_mix(csl, QS[oc])
            qsvm = b.val("vm")
            for h in range(NAH):
                hp, hh = h // 2, h % 2
                qmov = QS[hp][64 * hh:64 * hh + 64, :]
                sm_exp = {}

                def emit_av(t):
                    wt = [("sm", sm_exp[t])]
                    if t == 0:
                        wt += self.bank_wait("pmm2")
                    b.op("tensor", (lambda st_, m, ss, pp: lambda e: e.matmul(ps["pmm2"][0:65, :], st_, m, start=ss, stop=pp))(
                        VTP[hp * 16 + t][:, 65 * hh:65 * hh + 65], PBt[t % 2], t == 0, t == 15),
                        waits=wt, incs=[b.inc("tm", 1)])
                for tt in range(16):
                    pb = f"py{tt % 2}"
                    stat = K8[hp][64 * hh:64 * hh + 64, 128 * tt:128 * tt + 128]
                    wt0 = self.bank_wait(pb)
                    if h == 0 and tt == 0:
                        wt0 += [("vm", qsvm)]
                    b.op("tensor", (lambda o, s_, m: lambda e: e.matmul(o, s_, m, start=True, stop=True))(
                        ps[pb][:], stat, qmov),
                        waits=wt0, incs=[b.inc("tm", 1)])
                    b.op("scalar", (lambda o, p_: lambda e: e.activation(o, p_, AF.Exp, bias=0.0, scale=0.125))(
                        PBt[tt % 2], ps[pb][:]),
                        waits=[("tm", b.val("tm"))], incs=[b.inc("sm", 1)])
                    sm_exp[tt] = b.val("sm")
                    self.bank_set(pb, "sm")
                    if tt >= 1:
                        emit_av(tt - 1)
                emit_av(15)
                tm_av = b.val("tm")
                rrow = b.sb["R4b"][64:65, 0:512]
                b.op("vector", (lambda o: lambda e: e.reciprocal(o, ps["pmm2"][64:65, :]))(rrow),
                     waits=[("tm", tm_av)], incs=[b.inc("vm", 1)])
                b.op("vector", (lambda o: lambda e: e.tensor_copy(o, ps["pmm2"][0:64, :]))(yf),
                     incs=[b.inc("vm", 1)])
                self.bank_set("pmm2", "vm")
                b.op("tensor", lambda e: e.matmul(ps["pmm3"][0:64, :], b.sb["ones1"][64:65, 0:64], rrow, start=True, stop=True),
                     waits=[("vm", b.val("vm"))] + self.bank_wait("pmm3"), incs=[b.inc("tm", 1)])
                ydst = YP[hp][0:64, :] if hh == 0 else YHo[hp][0:64, :]
                b.op("vector", (lambda o: lambda e: e.tensor_tensor(o, yf, ps["pmm3"][0:64, :], ALU.mult))(ydst),
                     waits=[("tm", b.val("tm"))], incs=[b.inc("vm", 1)])
                self.bank_set("pmm3", "vm")
            vv = b.val("vm")
            b.op("sync", lambda e: e.dma_start(
                out=b.AB[64:128, YP.off:YP.off + 4096].rearrange("p (j c) -> p j c", c=512),
                in_=b.AB[0:64, YHo.off:YHo.off + 4096].rearrange("p (j c) -> p j c", c=512)),
                waits=[("vm", vv)], incs=[b.inc("xs", 16)])
            xsv = b.val("xs")
            for oc in range(8):
                slot, wst, wfn = b.wload((lambda oc_: lambda s_: dict(
                    out=s_[:, :1024].rearrange("p (dc o) -> p dc o", o=128),
                    in_=d["Wo"].ap().rearrange("(dc d) o -> d dc o", d=128)[:, :, oc_ * 128:(oc_ + 1) * 128]))(oc))
                bank = f"pmm{oc % 2}"
                for dc2 in range(DC):
                    wt = ([wst] + self.bank_wait(bank) + [("xs", xsv), ("vm", b.val("vm"))]) if dc2 == 0 else []
                    b.op("tensor", (lambda o, s_, m, st, sp: lambda e: e.matmul(o, s_, m, start=st, stop=sp))(
                        ps[bank][:], slot[:, dc2 * 128:(dc2 + 1) * 128], YP[dc2], dc2 == 0, dc2 == DC - 1),
                        waits=wt, incs=[b.inc("tm", 1)] if dc2 == DC - 1 else [])
                ta = raw[oc % 2]
                b.op("vector", (lambda t, p_, bc: lambda e: e.tensor_scalar(t[:, 0:512], p_, bc, None, ALU.add))(
                    ta, ps[bank][:], b.sb["bo"][:, oc:oc + 1]),
                    waits=[("tm", b.val("tm")), ("xs", b.val("xs"))], incs=[b.inc("vm", 1), b.inc(wfn, 1)])
                self.bank_set(bank, "vm")
                b.op("sync", (lambda a, t: lambda e: e.dma_start(out=a, in_=t[:, 0:512]))(
                    ao_d[oc * 128:(oc + 1) * 128, csl], ta),
                    waits=[("vm", b.val("vm"))], incs=[b.inc("xs", 16)])
        b.op("vector", lambda e: None, waits=[("xs", b.val("xs"))], incs=[b.inc("vm", 1)])

    def emit_moe(self, X1B, x1f_d, xo2_d):
        b, d = self.b, self.d
        ps = b.ps
        PH = self.PH
        H1 = Tiles(b.AB, PH + 16384, 32, 1024)
        ACC = Tiles(b.AB, PH + 49152, 8, 1024)
        CM8E = b.AF[0:8, 4112:6160]
        CM8 = b.AF[0:8, 6160:8208]
        G8 = b.AF[:, 2576:2584]
        P2 = b.AF[:, 2584:2592]
        M1 = b.AF[:, 2592:2593]
        M2 = b.AF[:, 2593:2594]
        REC = b.AF[:, 2594:2595]
        KP = b.AF[:, 2603:2611]
        Bq = [b.AF[:, 1040:1552], b.AF[:, 1552:2064]]
        tmp = b.AF[:, 2064:2576]
        xa = self.AFt(0, 2, 512)
        sq = b.AF[:, 2048:2560]
        AXL = mybir.AxisListType.X
        WgF = b.AF[:, 8208:8272]
        b.op("sync", lambda e: e.dma_start(out=WgF.rearrange("p (dc o) -> p dc o", o=8),
                                           in_=d["Wg"].ap().rearrange("(dc d) o -> d dc o", d=128)),
             incs=[b.inc("xs", 16)])
        for tc in range(NTCH):
            csl = slice(tc * TCH, (tc + 1) * TCH)
            for dc in range(DC):
                ta = xa[dc % 2]
                b.op("sync", (lambda a, t: lambda e: e.dma_start(out=t, in_=a))(
                    x1f_d[dc * 128:(dc + 1) * 128, csl], ta),
                    waits=[("vm", b.val("vm")), ("tm", b.val("tm"))], incs=[b.inc("xs", 16)])
                wt = [("xs", b.val("xs"))]
                if dc == 0:
                    wt += self.bank_wait("pst")
                b.op("tensor", (lambda s_, m, st, sp: lambda e: e.matmul(ps["pst"][0:8, :], s_, m, start=st, stop=sp))(
                    WgF[:, dc * 8:(dc + 1) * 8], ta, dc == 0, dc == DC - 1),
                    waits=wt, incs=[b.inc("tm", 1)])
            b.op("scalar", (lambda o: lambda e: e.activation(o, ps["pst"][0:8, :], AF.Exp, bias=b.sb["bg"][:], scale=1.0))(
                CM8E[:, csl]),
                waits=[("tm", b.val("tm")), ("vm", b.val("vm"))],
                incs=[b.inc("sm", 1)])
            self.bank_set("pst", "sm")
        for bb in range(16):
            c0 = bb * 128
            b.op("tensor", (lambda i_: lambda e: e.transpose(ps["pst"][:, 0:8], i_, b.sb["identf"][0:8, 0:8]))(
                CM8E[:, c0:c0 + 128]),
                waits=[("sm", b.val("sm"))] + self.bank_wait("pst"), incs=[b.inc("tm", 1)])
            b.op("vector", lambda e: e.tensor_copy(G8, ps["pst"][:, 0:8]),
                 waits=[("tm", b.val("tm"))], incs=[b.inc("vm", 1)])
            self.bank_set("pst", "vm")
            b.op("vector", lambda e: e.reduce_max(M1, G8, axis=AXL),
                 waits=[("vm", b.val("vm"))], incs=[b.inc("vm", 1)])
            b.op("vector", lambda e: e.tensor_scalar(KP, G8, M1, None, ALU.is_ge),
                 waits=[("vm", b.val("vm"))], incs=[b.inc("vm", 1)])
            b.op("vector", lambda e: e.scalar_tensor_tensor(P2, KP, b.sb["nbig"][:, 0:1], G8, ALU.mult, ALU.add),
                 waits=[("vm", b.val("vm"))], incs=[b.inc("vm", 1)])
            b.op("vector", lambda e: e.reduce_max(M2, P2, axis=AXL),
                 waits=[("vm", b.val("vm"))], incs=[b.inc("vm", 1)])
            b.op("vector", lambda e: e.tensor_scalar(KP, G8, M2, None, ALU.is_ge),
                 waits=[("vm", b.val("vm"))], incs=[b.inc("vm", 1)])
            b.op("vector", lambda e: e.tensor_tensor(M1, M1, M2, ALU.add),
                 waits=[("vm", b.val("vm"))], incs=[b.inc("vm", 1)])
            b.op("vector", lambda e: e.reciprocal(REC, M1),
                 waits=[("vm", b.val("vm"))], incs=[b.inc("vm", 1)])
            b.op("vector", lambda e: e.scalar_tensor_tensor(G8, G8, REC, KP, ALU.mult, ALU.mult),
                 waits=[("vm", b.val("vm"))], incs=[b.inc("vm", 1)])
            b.op("tensor", lambda e: e.transpose(ps["pst"][0:8, 8:136], G8, b.sb["identf"][:]),
                 waits=[("vm", b.val("vm"))] + self.bank_wait("pst"), incs=[b.inc("tm", 1)])
            b.op("vector", (lambda o: lambda e: e.tensor_copy(o, ps["pst"][0:8, 8:136]))(CM8[:, c0:c0 + 128]),
                 waits=[("tm", b.val("tm"))], incs=[b.inc("vm", 1)])
            self.bank_set("pst", "vm")
        cmvm = b.val("vm")
        if self.dev_stop == "comb":
            b.op("gpsimd", lambda e: e.dma_start(out=self.d["o_dev"].ap()[0:8, 0:2048], in_=CM8),
                 waits=[("vm", cmvm)], incs=[b.inc("xg", 16)])
            b.op("gpsimd", lambda e: e.dma_start(out=self.d["o_dev"].ap()[8:16, 0:2048], in_=CM8E),
                 waits=[("vm", cmvm)], incs=[b.inc("xg", 16)])
            self.devdump_barrier()
            return
        for half in range(2):
            th0 = half * 1024
            for dc in range(DC):
                for q in range(2):
                    c0 = th0 + q * 512
                    bank = f"pmm{2 + q}"
                    b.op("tensor", (lambda o, s_, m: lambda e: e.matmul(o, s_, m, start=True, stop=True))(
                        ps[bank][:], b.sb["b2b"][:, dc * 128:(dc + 1) * 128], CM8[:, c0:c0 + 512]),
                        waits=self.bank_wait(bank) + ([("vm", cmvm)] if dc == 0 and q == 0 else []),
                        incs=[b.inc("tm", 1)])
                    b.op("vector", (lambda o, x_, bk: lambda e: e.tensor_tensor(o, x_, ps[bk][:], ALU.add))(
                        ACC[dc][:, q * 512:(q + 1) * 512], X1B.sl(dc, c0, c0 + 512), bank),
                        waits=[("tm", b.val("tm"))], incs=[b.inc("vm", 1)])
                    if self.dev_stop == "minit" and half == 0 and dc == 0 and q == 0:
                        dd = self.d["o_dev"].ap()
                        b.op("vector", (lambda bk: lambda e: e.tensor_copy(tmp, ps[bk][:]))(bank),
                             incs=[b.inc("vm", 1)])
                        b.op("sync", lambda e: e.dma_start(out=dd[128:256, 0:512], in_=tmp),
                             waits=[("vm", b.val("vm"))], incs=[b.inc("xs", 16)])
                        b.op("sync", lambda e: e.dma_start(out=dd[384:392, 0:1024], in_=b.sb["b2b"][:]),
                             waits=[("vm", b.val("vm"))], incs=[b.inc("xs", 16)])
                        self.devdump_barrier()
                    self.bank_set(bank, "vm")
            if self.dev_stop == "minit" and half == 0:
                dd = self.d["o_dev"].ap()
                vv9 = b.val("vm")
                b.op("sync", lambda e: e.dma_start(out=dd[640:648, 0:2048], in_=CM8),
                     waits=[("vm", vv9)], incs=[b.inc("xs", 16)])
                b.op("gpsimd", (lambda a_: lambda e: e.dma_start(out=dd[512:640, 0:512], in_=a_))(
                    ACC[0][:, 0:512]),
                    waits=[("vm", vv9)], incs=[b.inc("xg", 16)])
                b.op("gpsimd", (lambda a_: lambda e: e.dma_start(out=dd[0:128, 0:512], in_=a_))(
                    X1B.sl(0, 0, 512)),
                    waits=[("vm", vv9)], incs=[b.inc("xg", 16)])
                self.devdump_barrier()
            for e_ in range(NEXP):
                for fc in range(FFC):
                    slot, wst, wfn = b.wload((lambda e2, fc_: lambda s: dict(
                        out=s[:, :1024].rearrange("p (dc f) -> p dc f", f=128),
                        in_=d["W1"].ap().rearrange("(a d) f -> d a f", d=128)[:, 8 * e2:8 * e2 + 8, fc_ * 128:(fc_ + 1) * 128]))(e_, fc))
                    for q in range(2):
                        bank = f"pmm{q}"
                        c0 = th0 + q * 512
                        for dc in range(DC):
                            wt = []
                            if dc == 0:
                                wt = self.bank_wait(bank) + ([wst] if q == 0 else [])
                            b.op("tensor", (lambda s_, m, bk, st, sp: lambda e: e.matmul(ps[bk][:], s_, m, start=st, stop=sp))(
                                slot[:, dc * 128:(dc + 1) * 128], X1B.sl(dc, c0, c0 + 512), bank, dc == 0, dc == DC - 1),
                                waits=wt, incs=[b.inc("tm", 1)] if dc == DC - 1 else [])
                        b.op("scalar", (lambda o, bk, bc: lambda e: e.activation(o, ps[bk][:], AF.Gelu, bias=bc, scale=1.0))(
                            H1[fc][:, q * 512:(q + 1) * 512], bank, b.sb["b1t"][:, e_ * 32 + fc:e_ * 32 + fc + 1]),
                            waits=[("tm", b.val("tm"))],
                            incs=[b.inc("sm", 1)] + ([b.inc(wfn, 1)] if q == 1 else []))
                        self.bank_set(bank, "sm")
                for q in range(2):
                    c0 = th0 + q * 512
                    b.op("tensor", (lambda s_, m: lambda e: e.matmul(ps["pst"][:], s_, m, start=True, stop=True))(
                        b.sb["OH8"][:, e_ * 128:(e_ + 1) * 128], CM8[:, c0:c0 + 512]),
                        waits=self.bank_wait("pst"), incs=[b.inc("tm", 1)])
                    b.op("vector", (lambda o: lambda e: e.tensor_copy(o, ps["pst"][:]))(Bq[q]),
                         waits=[("tm", b.val("tm"))], incs=[b.inc("vm", 1)])
                    self.bank_set("pst", "vm")
                for dc in range(DC):
                    sl1, wst1, wfn1 = b.wload((lambda e2, dc_: lambda s: dict(
                        out=s[:, :2048].rearrange("p (a c) -> p a c", a=1),
                        in_=d["W2"].ap().rearrange("(a f) c -> f a c", f=128)[:, 8 * e2 + dc_:8 * e2 + dc_ + 1, 0:2048]))(e_, dc))
                    sl2, wst2, wfn2 = b.wload((lambda e2, dc_: lambda s: dict(
                        out=s[:, :2048].rearrange("p (a c) -> p a c", a=1),
                        in_=d["W2"].ap().rearrange("(a f) c -> f a c", f=128)[:, 8 * e2 + dc_:8 * e2 + dc_ + 1, 2048:4096]))(e_, dc))
                    for q in range(2):
                        bank = f"pmm{q}"
                        for fc in range(FFC):
                            sl = sl1 if fc < 16 else sl2
                            wt = []
                            if fc == 0:
                                wt = self.bank_wait(bank) + ([wst1, ("sm", b.val("sm"))] if q == 0 else [])
                            elif fc == 16 and q == 0:
                                wt = [wst2]
                            b.op("tensor", (lambda s_, m, bk, st, sp: lambda e: e.matmul(ps[bk][:], s_, m, start=st, stop=sp))(
                                sl[:, (fc % 16) * 128:(fc % 16 + 1) * 128], H1[fc][:, q * 512:(q + 1) * 512], bank, fc == 0, fc == FFC - 1),
                                waits=wt, incs=[b.inc("tm", 1)] if fc == FFC - 1 else [])
                        tmv = b.val("tm")
                        b.op("vector", (lambda bk, bq_: lambda e: e.tensor_tensor(tmp, ps[bk][:], bq_, ALU.mult))(bank, Bq[q]),
                             waits=[("tm", tmv)],
                             incs=[b.inc("vm", 1)] + ([b.inc(wfn1, 1), b.inc(wfn2, 1)] if q == 1 else []))
                        self.bank_set(bank, "vm")
                        b.op("vector", (lambda o: lambda e: e.tensor_tensor(o, o, tmp, ALU.add))(
                            ACC[dc][:, q * 512:(q + 1) * 512]), incs=[b.inc("vm", 1)])
                        if self.dev_stop == "mexp" and half == 0 and e_ == 0 and dc == 0 and q == 0:
                            dd = self.d["o_dev"].ap()
                            vv9 = b.val("vm")
                            b.op("sync", lambda e: e.dma_start(out=dd[0:128, 0:512], in_=tmp),
                                 waits=[("vm", vv9)], incs=[b.inc("xs", 16)])
                            b.op("sync", (lambda bq_: lambda e: e.dma_start(out=dd[128:256, 0:512], in_=bq_))(Bq[0]),
                                 waits=[("vm", vv9)], incs=[b.inc("xs", 16)])
                            b.op("gpsimd", lambda e: e.dma_start(out=dd[256:384, 0:1024], in_=H1[0]),
                                 waits=[("vm", vv9)], incs=[b.inc("xg", 16)])
                            b.op("gpsimd", lambda e: e.dma_start(out=dd[384:512, 0:1024], in_=H1[31]),
                                 waits=[("vm", vv9)], incs=[b.inc("xg", 16)])
                            b.op("gpsimd", (lambda a_: lambda e: e.dma_start(out=dd[512:640, 0:512], in_=a_))(
                                ACC[0][:, 0:512]),
                                waits=[("vm", vv9)], incs=[b.inc("xg", 16)])
                            b.op("sync", lambda e: e.dma_start(out=dd[640:648, 0:2048], in_=CM8),
                                 waits=[("vm", vv9)], incs=[b.inc("xs", 16)])
                            self.devdump_barrier()
            if self.dev_stop == "macc":
                self.devdump_tiles(ACC, 8, 0, th0, w=1024)
                self.devdump_barrier()
                continue
            tr0 = b.sb["R4b"][0:1, 0:512]
            tr1 = b.sb["R4b"][0:1, 512:1024]
            tr2 = b.sb["R4b"][0:1, 1024:1536]
            for q in range(2):
                c0 = q * 512
                absl = slice(th0 + c0, th0 + c0 + 512)
                accvm = b.val("vm")
                for dc in range(DC):
                    acsl = ACC[dc][:, c0:c0 + 512]
                    b.op("vector", (lambda a_: lambda e: e.tensor_tensor(sq, a_, a_, ALU.mult))(acsl),
                         waits=[("tm", b.val("tm"))], incs=[b.inc("vm", 1)])
                    vmv = b.val("vm")
                    b.op("tensor", (lambda a_, st, sp: lambda e: e.matmul(ps["py0"][0:1, :], b.sb["onesb"][:], a_, start=st, stop=sp))(
                        acsl, dc == 0, dc == DC - 1),
                        waits=[("vm", vmv)] + (self.bank_wait("py0") if dc == 0 else []), incs=[b.inc("tm", 1)])
                    b.op("tensor", (lambda st, sp: lambda e: e.matmul(ps["py1"][0:1, :], b.sb["onesf"][:], sq, start=st, stop=sp))(
                        dc == 0, dc == DC - 1),
                        waits=[("vm", vmv)] + (self.bank_wait("py1") if dc == 0 else []), incs=[b.inc("tm", 1)])
                tmv = b.val("tm")
                b.op("vector", lambda e: e.tensor_scalar(tr0, ps["py0"][0:1, :], 1.0 / D, None, ALU.mult),
                     waits=[("tm", tmv)], incs=[b.inc("vm", 1)])
                b.op("vector", lambda e: e.tensor_scalar(tr1, ps["py1"][0:1, :], 1.0 / D, None, ALU.mult),
                     incs=[b.inc("vm", 1)])
                b.op("vector", lambda e: e.tensor_tensor(tr2, tr0, tr0, ALU.mult), incs=[b.inc("vm", 1)])
                b.op("vector", lambda e: e.tensor_tensor(tr1, tr1, tr2, ALU.subtract), incs=[b.inc("vm", 1)])
                self.bank_set("py0", "vm")
                self.bank_set("py1", "vm")
                b.op("scalar", lambda e: e.activation(tr2, tr1, AF.Ln, bias=b.sb["eps5"][:], scale=1.0),
                     waits=[("vm", b.val("vm"))], incs=[b.inc("sm", 1)])
                b.op("scalar", lambda e: e.activation(tr1, tr2, AF.Exp, bias=0.0, scale=-0.5),
                     incs=[b.inc("sm", 1)])
                smv = b.val("sm")
                b.op("tensor", lambda e: e.matmul(ps["pst"][:], b.sb["ones1"][0:1, :], tr0, start=True, stop=True),
                     waits=[("sm", smv)] + self.bank_wait("pst"), incs=[b.inc("tm", 1)])
                b.op("tensor", lambda e: e.matmul(ps["py0"][:], b.sb["ones1"][0:1, :], tr1, start=True, stop=True),
                     waits=[("sm", smv)] + self.bank_wait("py0"), incs=[b.inc("tm", 1)])
                tmb = b.val("tm")
                for dc in range(DC):
                    rsl = slice(dc * 128, (dc + 1) * 128)
                    ta = xa[dc % 2]
                    b.op("sync", (lambda a, t: lambda e: e.dma_start(out=t, in_=a))(xo2_d[rsl, absl], ta),
                         waits=[("vm", b.val("vm")), ("tm", b.val("tm"))], incs=[b.inc("xs", 16)])
                    b.op("vector", (lambda a_: lambda e: e.tensor_tensor(sq, a_, ps["pst"][:], ALU.subtract))(
                        ACC[dc][:, c0:c0 + 512]),
                        waits=[("tm", tmb), ("xs", b.val("xs"))], incs=[b.inc("vm", 1)])
                    b.op("vector", lambda e: e.tensor_tensor(sq, sq, ps["py0"][:], ALU.mult),
                         incs=[b.inc("vm", 1)])
                    b.op("vector", (lambda g_, be_: lambda e: e.tensor_scalar(sq, sq, g_, be_, ALU.mult, ALU.add))(
                        b.sb["g2"][:, dc:dc + 1], b.sb["be2"][:, dc:dc + 1]),
                        incs=[b.inc("vm", 1)])
                    b.op("vector", (lambda t: lambda e: e.tensor_tensor(t, t, sq, ALU.add))(ta),
                         incs=[b.inc("vm", 1)])
                    b.op("sync", (lambda a, t: lambda e: e.dma_start(out=a, in_=t))(xo2_d[rsl, absl], ta),
                         waits=[("vm", b.val("vm"))], incs=[b.inc("xs", 16)])
                self.bank_set("pst", "vm")
                self.bank_set("py0", "vm")
        b.op("vector", lambda e: None, waits=[("xs", b.val("xs"))], incs=[b.inc("vm", 1)])

    def emit_qkvb(self, H3, q_d):
        b, d = self.b, self.d
        ps = b.ps
        raw = self.AFt(0, 2, 520)
        qa = b.AF[:, 1040:1552]
        qb = b.AF[:, 1552:2064]
        rc2 = b.AF[0:2, 2064:2576]
        for tc in range(NTCH):
            t0 = tc * TCH
            csl = slice(t0, t0 + TCH)
            for oc in range(25):
                ow = 128 if oc < 24 else 16
                o0 = oc * 128
                slot, wst, wfn = b.wload((lambda o0_, ow_: lambda s: dict(
                    out=s[:, :8 * ow_].rearrange("p (dc o) -> p dc o", o=ow_),
                    in_=d["w_qkvb"].ap().rearrange("(dc d) o -> d dc o", d=128)[:, :, o0_:o0_ + ow_]))(o0, ow))
                bank = f"pmm{oc % 2}"
                pout = ps[bank][:ow, :]
                for dc in range(DC):
                    wt = ([wst] + self.bank_wait(bank)) if dc == 0 else []
                    if tc == 0 and oc == 0 and dc == 0:
                        wt += [("vm", b.val("vm"))]
                    b.op("tensor", (lambda o, s_, m, st, sp: lambda e: e.matmul(o, s_, m, start=st, stop=sp))(
                        pout, slot[:, dc * ow:(dc + 1) * ow], H3.sl(dc, t0, t0 + TCH), dc == 0, dc == DC - 1),
                        waits=wt, incs=[b.inc("tm", 1)] if dc == DC - 1 else [])
                tmv = b.val("tm")
                ta = raw[oc % 2]
                if oc < 16:
                    b.op("vector", (lambda p_: lambda e: e.tensor_scalar(qa, p_, 0.0, None, ALU.min))(pout),
                         waits=[("tm", tmv), ("xs", b.val("xs"))], incs=[b.inc("vm", 1)])
                    b.op("scalar", lambda e: e.activation(qb, qa, AF.Exp, bias=0.0, scale=1.0),
                         waits=[("vm", b.val("vm"))], incs=[b.inc("sm", 1)])
                    b.op("vector", (lambda p_: lambda e: e.tensor_scalar(qa, p_, 0.0, None, ALU.max))(pout),
                         waits=[("sm", b.val("sm"))], incs=[b.inc("vm", 1), b.inc(wfn, 1)])
                    self.bank_set(bank, "vm")
                    b.op("vector", lambda e: e.tensor_tensor(qa, qa, qb, ALU.add),
                         waits=[("sm", b.val("sm"))], incs=[b.inc("vm", 1)])
                    b.op("tensor", lambda e: e.matmul(ps["pst"][0:2, :], b.sb["sel64"][:], qa, start=True, stop=True),
                         waits=[("vm", b.val("vm"))] + self.bank_wait("pst"), incs=[b.inc("tm", 1)])
                    b.op("vector", lambda e: e.reciprocal(rc2, ps["pst"][0:2, :]),
                         waits=[("tm", b.val("tm"))], incs=[b.inc("vm", 1)])
                    self.bank_set("pst", "vm")
                    b.op("tensor", lambda e: e.matmul(ps["pst"][:], b.sb["bc64"][:], rc2, start=True, stop=True),
                         waits=[("vm", b.val("vm"))] + self.bank_wait("pst"), incs=[b.inc("tm", 1)])
                    b.op("vector", (lambda t: lambda e: e.tensor_tensor(t[:, 0:512], qa, ps["pst"][:], ALU.mult))(ta),
                         waits=[("tm", b.val("tm"))], incs=[b.inc("vm", 1)])
                    self.bank_set("pst", "vm")
                elif oc < 24:
                    b.op("vector", (lambda t, p_: lambda e: e.tensor_copy(t[:, 0:512], p_))(ta, pout),
                         waits=[("tm", tmv), ("xs", b.val("xs"))], incs=[b.inc("vm", 1), b.inc(wfn, 1)])
                    self.bank_set(bank, "vm")
                else:
                    b.op("scalar", (lambda t, p_: lambda e: e.activation(t[0:16, 0:512], p_, AF.Sigmoid, bias=0.0, scale=1.0))(ta, pout),
                         waits=[("tm", tmv), ("xs", b.val("xs"))], incs=[b.inc("sm", 1), b.inc(wfn, 1)])
                    self.bank_set(bank, "sm")
                    b.op("vector", lambda e: None, waits=[("sm", b.val("sm"))], incs=[b.inc("vm", 1)])
                b.op("sync", (lambda a, t, ow_: lambda e: e.dma_start(out=a, in_=t[0:ow_, 0:512]))(
                    q_d[o0:o0 + ow, csl], ta, ow),
                    waits=[("vm", b.val("vm"))], incs=[b.inc("xs", 16)])
        b.op("vector", lambda e: None, waits=[("xs", b.val("xs"))], incs=[b.inc("vm", 1)])

    def emit_wo_b(self, lo_d, ao_d):
        b, d = self.b, self.d
        ps = b.ps
        PH = self.PH
        LOS = Tiles(b.AB, PH, 8, 512)
        raw = self.AFt(0, 2, 520)
        for tc in range(NTCH):
            t0 = tc * TCH
            csl = slice(t0, t0 + TCH)
            b.op("gpsimd", (lambda c_: lambda e: e.dma_start(
                out=b.AB[:, PH:PH + 4096].rearrange("p (dc t) -> p dc t", t=512),
                in_=lo_d.rearrange("(dc d) t -> d dc t", d=128)[:, :, c_]))(csl),
                waits=[("tm", b.val("tm"))], incs=[b.inc("xg", 16)])
            b.op("vector", lambda e: None, waits=[("xg", b.val("xg")), ("xs", b.val("xs"))], incs=[b.inc("vm", 1)])
            for oc in range(8):
                slot, wst, wfn = b.wload((lambda oc_: lambda s_: dict(
                    out=s_[:, :1024].rearrange("p (dc o) -> p dc o", o=128),
                    in_=d["w_ob"].ap().rearrange("(dc d) o -> d dc o", d=128)[:, :, oc_ * 128:(oc_ + 1) * 128]))(oc))
                bank = f"pmm{oc % 2}"
                for dc in range(DC):
                    wt = ([wst] + self.bank_wait(bank) + [("vm", b.val("vm"))]) if dc == 0 else []
                    b.op("tensor", (lambda o, s_, m, st, sp: lambda e: e.matmul(o, s_, m, start=st, stop=sp))(
                        ps[bank][:], slot[:, dc * 128:(dc + 1) * 128], LOS[dc], dc == 0, dc == DC - 1),
                        waits=wt, incs=[b.inc("tm", 1)] if dc == DC - 1 else [])
                ta = raw[oc % 2]
                b.op("vector", (lambda t, p_: lambda e: e.tensor_copy(t[:, 0:512], p_))(ta, ps[bank][:]),
                     waits=[("tm", b.val("tm")), ("xs", b.val("xs"))], incs=[b.inc("vm", 1), b.inc(wfn, 1)])
                self.bank_set(bank, "vm")
                b.op("sync", (lambda a, t: lambda e: e.dma_start(out=a, in_=t[:, 0:512]))(
                    ao_d[oc * 128:(oc + 1) * 128, csl], ta),
                    waits=[("vm", b.val("vm"))], incs=[b.inc("xs", 16)])
        b.op("vector", lambda e: None, waits=[("xs", b.val("xs"))], incs=[b.inc("vm", 1)])

    def devdump_tiles(self, tiles, n, row0, csl0, w=512):
        b, d = self.b, self.d
        for i in range(n):
            b.op("gpsimd", (lambda a, t: lambda e: e.dma_start(out=a, in_=t))(
                d["o_dev"].ap()[row0 + i * 128:row0 + (i + 1) * 128, csl0:csl0 + w], tiles[i]),
                waits=[("vm", b.val("vm")), ("sm", b.val("sm"))], incs=[b.inc("xg", 16)])

    def devdump_barrier(self):
        b = self.b
        xv = b.val("xg")
        xsv = b.val("xs")
        for eng, sem in (("scalar", "sm"), ("vector", "vm"), ("tensor", "tm")):
            b.op(eng, lambda e: None, waits=[("xg", xv), ("xs", xsv)], incs=[b.inc(sem, 1)])

    def emit_dram_copy(self, dst, src, rows, cols=T):
        b = self.b
        b.op("sync", (lambda a, t: lambda e: e.dma_start(out=a, in_=t))(dst, src),
             waits=[("xs", b.val("xs")), ("xg", b.val("xg")), ("vm", b.val("vm"))],
             incs=[b.inc("so", 16)])

    def finish(self):
        b = self.b
        b.op("sync", lambda e: None, waits=[("so", b.val("so")), ("xs", b.val("xs"))])

    def build(self, emit_fn):
        from contextlib import ExitStack
        nc = self.nc
        with ExitStack() as ctx:
            self.alloc(ctx)
            block = ctx.enter_context(nc.Block())
            emit_fn()
            self.b.flush(block)
        return nc


def build_l3(dev_stop=None):
    p = Prog(dev_stop, launch=3)
    b, d = p.b, p.d

    def emit():
        p.emit_consts()
        p.emit_wo_b(d["lo_in"].ap(), d["dT1"].ap())
        p.emit_boundary(d["h3_in"].ap(), d["dT1"].ap(), d["dT1"].ap(), d["dX1"].ap(), "eps5",
                        gain="lng", bias="lnb", post_add=d["x3_in"].ap())
        if dev_stop == "l3a":
            p.emit_dram_copy(d["o_dev"].ap()[0:D, :], d["dX1"].ap(), D)
            p.finish()
            return
        p.emit_boundary(d["dX1"].ap(), None, None, d["dU"].ap(), "eps5")
        p.emit_fattn(d["dU"].ap(), d["dT1"].ap())
        X1B = Tiles(b.AB, p.PH, 8, 2048)
        p.emit_boundary(d["dU"].ap(), d["dT1"].ap(), d["dT1"].ap(), d["dU"].ap(), "eps5",
                        bf_dst=X1B, gain="g1", bias="be1")
        p.emit_moe(X1B, d["dU"].ap(), d["dX1"].ap())
        p.emit_dram_copy(d["o_x"].ap(), d["dX1"].ap(), D)
        p.finish()

    nc = p.build(emit)
    return nc


GRAM_KK = [(u, w) for u in range(1, 8) for w in range(u)]
GRAM_QK = [(s, u) for s in range(8) for u in range(s + 1)]


def host_l2_consts():
    ohg = np.zeros((64, 64 * 64), np.float32)
    for r in range(64):
        val = -1.0 if r < len(GRAM_KK) else 0.125
        ohg[:, r * 64 + r] = val
    return ohg


def build_l2():
    nc = bass.Bass("TRN2", target_bir_lowering=False, debug=False)
    b = Bld(nc)
    d = {}
    dr = lambda name, shape, dt=F32, kind=None: d.__setitem__(
        name, nc.dram_tensor(name, shape, dt, kind=kind) if kind else nc.dram_tensor(name, shape, dt))
    dr("kd", [64, 8 * 4096], F32, "ExternalInput")
    dr("qd", [64, 8 * 4096], F32, "ExternalInput")
    dr("vp", [4096, 512], F32, "ExternalInput")
    dr("bp", [4096, 8], F32, "ExternalInput")
    dr("ohg", [64, 4096], F32, "ExternalInput")
    dr("id128", [128, 128], F32, "ExternalInput")
    dr("op", [4096, 512], F32, "ExternalOutput")
    from contextlib import ExitStack
    with ExitStack() as ctx:
        KD = ctx.enter_context(nc.sbuf_tensor("KD", [64, 32768], BF16))
        QD = ctx.enter_context(nc.sbuf_tensor("QD", [64, 32768], BF16))
        PRD = ctx.enter_context(nc.sbuf_tensor("PRD", [64, 4096], BF16))
        AG = ctx.enter_context(nc.sbuf_tensor("AG", [64, 4096], F32))
        OHG = ctx.enter_context(nc.sbuf_tensor("OHG", [64, 4096], BF16))
        ID128 = ctx.enter_context(nc.sbuf_tensor("ID128", [128, 128], F32))
        AT = ctx.enter_context(nc.sbuf_tensor("AT", [128, 128], F32))
        VP = ctx.enter_context(nc.sbuf_tensor("VP", [128, 1024], F32))
        BP = ctx.enter_context(nc.sbuf_tensor("BP", [128, 16], F32))
        VI = ctx.enter_context(nc.sbuf_tensor("VI", [128, 512], F32))
        ACC = ctx.enter_context(nc.sbuf_tensor("ACCL", [128, 64], F32))
        OUT = ctx.enter_context(nc.sbuf_tensor("OUTL", [128, 1024], F32))
        ps = {}
        for i in range(4):
            ps[f"pa{i}"] = ctx.enter_context(nc.psum_tensor(f"pa{i}", [64, 512], F32))
        pt = ctx.enter_context(nc.psum_tensor("pt", [128, 128], F32))
        own_sems = ["own_vector", "own_scalar", "own_tensor"] if Bld.SIMSAFE else []
        for s in ["tm", "vm", "sm", "xs", "xg", "so"] + own_sems:
            b.H[s] = ctx.enter_context(nc.semaphore("sem_" + s))
            b.n.setdefault(s, 0)
        block = ctx.enter_context(nc.Block())
        bankg = {"pt": ("vm", 0)}

        b.op("gpsimd", lambda e: e.dma_start(out=KD[:], in_=d["kd"].ap()), incs=[b.inc("xg", 16)])
        b.op("gpsimd", lambda e: e.dma_start(out=QD[:], in_=d["qd"].ap()), incs=[b.inc("xg", 16)])
        b.op("gpsimd", lambda e: e.dma_start(out=OHG[:], in_=d["ohg"].ap()), incs=[b.inc("xg", 16)])
        b.op("sync", lambda e: e.dma_start(out=ID128[:], in_=d["id128"].ap()), incs=[b.inc("xs", 16)])
        ldg0 = b.val("xg")
        ldx = b.val("xs")
        grams = [("k", u, w) for (u, w) in GRAM_KK] + [("q", s, u) for (s, u) in GRAM_QK]
        evacvm = 0
        for half2 in range(2):
            ph0 = half2 * 2048
            for r, (kind, a1, a2) in enumerate(grams):
                pr = PRD[:, (r % 2) * 2048:(r % 2) * 2048 + 2048]
                in0 = (QD if kind == "q" else KD)[:, a1 * 4096 + ph0:a1 * 4096 + ph0 + 2048]
                in1 = KD[:, a2 * 4096 + ph0:a2 * 4096 + ph0 + 2048]
                wt = [("xg", ldg0)] if (half2 == 0 and r < 2) else [("tm", b.val("tm") - 4)]
                b.op("vector", (lambda o, i0, i1: lambda e: e.tensor_tensor(o, i0, i1, ALU.mult))(pr, in0, in1),
                     waits=wt, incs=[b.inc("vm", 1)])
                vmv = b.val("vm")
                for c in range(4):
                    wtc = [("vm", vmv)] if c == 0 else []
                    if half2 == 1 and r == 0 and c == 0:
                        wtc += [("vm", evacvm)]
                    b.op("tensor", (lambda o, s_, m, st, sp: lambda e: e.matmul(o, s_, m, start=st, stop=sp))(
                        ps[f"pa{c}"][:], OHG[:, r * 64:(r + 1) * 64], pr[:, c * 512:(c + 1) * 512],
                        r == 0, r == 63),
                        waits=wtc, incs=[b.inc("tm", 1)])
            tmv = b.val("tm")
            for c in range(4):
                b.op("vector", (lambda c_, p0_: lambda e: e.tensor_copy(AG[:, p0_ + c_ * 512:p0_ + (c_ + 1) * 512], ps[f"pa{c_}"][:]))(c, ph0),
                     waits=[("tm", tmv)] if c == 0 else [], incs=[b.inc("vm", 1)])
            evacvm = b.val("vm")
        agv = b.val("vm")
        NKK = len(GRAM_KK)
        for g in range(32):
            gb = g % 2
            p0 = g * 128
            b.op("sync", (lambda g_, gb_: lambda e: e.dma_start(out=VP[:, gb_ * 512:gb_ * 512 + 512],
                                                               in_=d["vp"].ap()[g_ * 128:(g_ + 1) * 128, :]))(g, gb),
                 waits=[("vm", b.val("vm"))], incs=[b.inc("xs", 16)])
            b.op("sync", (lambda g_, gb_: lambda e: e.dma_start(out=BP[:, gb_ * 8:gb_ * 8 + 8],
                                                               in_=d["bp"].ap()[g_ * 128:(g_ + 1) * 128, :]))(g, gb),
                 incs=[b.inc("xs", 16)])
            ldg = b.val("xs")
            wtt = [("vm", agv)] if g == 0 else []
            b.op("tensor", (lambda p0_: lambda e: e.transpose(pt[:, 0:64], AG[:, p0_:p0_ + 128], ID128[0:64, 0:64]))(p0),
                 waits=wtt + [(bankg["pt"][0], bankg["pt"][1])] + ([("xs", ldx)] if g == 0 else []),
                 incs=[b.inc("tm", 1)])
            b.op("vector", (lambda gb_: lambda e: e.tensor_copy(AT[:, gb_ * 64:gb_ * 64 + 64], pt[:, 0:64]))(gb),
                 waits=[("tm", b.val("tm"))], incs=[b.inc("vm", 1)])
            bankg["pt"] = ("vm", b.val("vm"))
            atc = lambda r, gb_=gb: AT[:, gb_ * 64 + r:gb_ * 64 + r + 1]
            vpt = VP[:, gb * 512:gb * 512 + 512]
            bpt = BP[:, gb * 8:gb * 8 + 8]
            outt = OUT[:, gb * 512:gb * 512 + 512]
            kkidx = {uw: r for r, uw in enumerate(GRAM_KK)}
            qkidx = {su: NKK + r for r, su in enumerate(GRAM_QK)}
            for u in range(8):
                if u == 0:
                    b.op("vector", (lambda v_, bc: lambda e: e.tensor_scalar(VI[:, 0:64], v_, bc, None, ALU.mult))(
                        vpt[:, 0:64], bpt[:, 0:1]),
                        waits=[("vm", b.val("vm")), ("xs", ldg)], incs=[b.inc("vm", 1)])
                else:
                    b.op("vector", (lambda v_: lambda e: e.tensor_copy(ACC[:], v_))(vpt[:, u * 64:u * 64 + 64]),
                         waits=[("vm", b.val("vm"))], incs=[b.inc("vm", 1)])
                    for w in range(u):
                        b.op("vector", (lambda w_, a_: lambda e: e.scalar_tensor_tensor(
                            ACC[:], VI[:, w_ * 64:w_ * 64 + 64], a_, ACC[:], ALU.mult, ALU.add))(w, atc(kkidx[(u, w)])),
                            waits=[("vm", b.val("vm"))], incs=[b.inc("vm", 1)])
                    b.op("vector", (lambda u_, bc: lambda e: e.tensor_scalar(VI[:, u_ * 64:u_ * 64 + 64], ACC[:], bc, None, ALU.mult))(
                        u, bpt[:, u:u + 1]),
                        waits=[("vm", b.val("vm"))], incs=[b.inc("vm", 1)])
            for s in range(8):
                for u in range(s + 1):
                    if u == 0:
                        b.op("vector", (lambda o_, a_: lambda e: e.tensor_scalar(
                            o_, VI[:, 0:64], a_, None, ALU.mult))(outt[:, s * 64:s * 64 + 64], atc(qkidx[(s, 0)])),
                            waits=[("vm", b.val("vm"))], incs=[b.inc("vm", 1)])
                    else:
                        b.op("vector", (lambda o_, u_, a_: lambda e: e.scalar_tensor_tensor(
                            o_, VI[:, u_ * 64:u_ * 64 + 64], a_,
                            o_, ALU.mult, ALU.add))(outt[:, s * 64:s * 64 + 64], u, atc(qkidx[(s, u)])),
                            waits=[("vm", b.val("vm"))], incs=[b.inc("vm", 1)])
            b.op("sync", (lambda g_, gb_: lambda e: e.dma_start(out=d["op"].ap()[g_ * 128:(g_ + 1) * 128, :],
                                                               in_=OUT[:, gb_ * 512:gb_ * 512 + 512]))(g, gb),
                 waits=[("vm", b.val("vm"))], incs=[b.inc("xs", 16)])
        b.op("sync", lambda e: None, waits=[("xs", b.val("xs"))])
        b.flush(block)
    return nc


def build_l1(dev_stop=None):
    p = Prog(dev_stop)
    b, d = p.b, p.d

    def emit():
        p.emit_consts()
        RH = Tiles(b.AB, p.PH, 8, 2048)
        p.emit_boundary(d["x_t"].ap(), None, None, d["dU"].ap(), eps_tile="eps5")
        if dev_stop == "ln0":
            p.emit_dram_copy(d["o_dev"].ap()[0:D, :], d["dU"].ap(), D)
            p.finish()
            return
        p.emit_rms_bf(d["dU"].ap(), RH, "eps6")
        p.emit_mamba(RH, d["dU"].ap())
        if dev_stop == "mamba":
            p.emit_dram_copy(d["o_dev"].ap()[0:D, :], d["dU"].ap(), D)
            p.finish()
            return
        p.emit_rms_bf(d["dU"].ap(), RH, "eps6")
        p.emit_mqa(RH, d["dU"].ap())
        if dev_stop == "qkv":
            p.finish()
            return
        if dev_stop == "mqa":
            p.emit_dram_copy(d["o_dev"].ap()[0:D, :], d["dU"].ap(), D)
            p.finish()
            return
        p.emit_rms_bf(d["dU"].ap(), RH, "eps6")
        p.emit_rwkv(RH, d["dU"].ap())
        if dev_stop == "rwkv":
            p.emit_dram_copy(d["o_dev"].ap()[0:D, :], d["dU"].ap(), D)
            p.finish()
            return
        p.emit_boundary(d["x_t"].ap(), d["dU"].ap(), d["dX1"].ap(), d["dU"].ap(), "eps5")
        p.emit_fattn(d["dU"].ap(), d["dT1"].ap())
        if dev_stop == "fattn":
            p.emit_dram_copy(d["o_dev"].ap()[0:D, :], d["dT1"].ap(), D)
            p.finish()
            return
        X1B = Tiles(b.AB, p.PH, 8, 2048)
        p.emit_boundary(d["dU"].ap(), d["dT1"].ap(), d["dT1"].ap(), d["dU"].ap(), "eps5",
                        bf_dst=X1B, gain="g1", bias="be1")
        if dev_stop == "x1":
            p.devdump_tiles(X1B, 8, 0, 0, w=2048)
            p.devdump_barrier()
            p.finish()
            return
        p.emit_moe(X1B, d["dU"].ap(), d["dX1"].ap())
        if dev_stop in ("comb", "macc", "mexp", "minit"):
            p.finish()
            return
        if dev_stop == "ffn":
            p.emit_dram_copy(d["o_dev"].ap()[0:D, :], d["dX1"].ap(), D)
            p.finish()
            return
        H3 = Tiles(b.AB, p.PH + 16384, 8, 2048)
        p.emit_boundary(d["dX1"].ap(), None, None, d["o_h"].ap(), "eps5", bf_dst=H3)
        p.emit_qkvb(H3, d["o_q"].ap())
        p.emit_dram_copy(d["o_x"].ap(), d["dX1"].ap(), D)
        if dev_stop == "qkvb2":
            p.emit_dram_copy(d["o_dev"].ap()[0:3088, :], d["o_q"].ap(), 3088)
            p.finish()
            return
        p.finish()

    nc = p.build(emit)
    return nc


WKEYS = ("w_in", "w_out", "w_qkv", "w_cproj", "w_key", "w_rec", "w_val",
         "Wq", "Wk", "Wv", "Wo", "Wg", "W1", "W2", "w_qkvb")


def kernel(**inputs):
    from concourse.bass_utils import run_bass_kernel_spmd
    w = host_prep(inputs)
    c = host_consts()
    x = np.ascontiguousarray(np.asarray(inputs["L0_attn_a_x"], dtype=np.float32))
    base = {}
    for name, hk, shape, dt in CONST_SPECS:
        base["c_" + name] = np.ascontiguousarray(w[hk] if hk in w else c[hk])
    base["ropeC"], base["ropeS"] = c["ropeC"], c["ropeS"]
    base["identf"] = np.eye(128, dtype=np.float32)
    maps1 = []
    for core in range(NCORES):
        m = dict(base)
        for k in WKEYS:
            m[k] = w[k]
        m["x_t"] = np.ascontiguousarray(x[core].T)
        maps1.append(m)
    nc1 = build_l1()
    r1 = run_bass_kernel_spmd(nc1, maps1, core_ids=list(range(NCORES)))
    oq = np.stack([np.asarray(r1.results[s]["o_q"]) for s in range(NCORES)])
    ox3 = [np.ascontiguousarray(np.asarray(r1.results[s]["o_x"])) for s in range(NCORES)]
    oh3 = [np.ascontiguousarray(np.asarray(r1.results[s]["o_h"])) for s in range(NCORES)]
    qn = oq[:, 0:1024, :].reshape(8, 16, 64, 2048)
    kn = oq[:, 1024:2048, :].reshape(8, 16, 64, 2048)
    vv = oq[:, 2048:3072, :].reshape(8, 16, 64, 2048)
    bb_ = oq[:, 3072:3088, :]
    ohg = host_l2_consts()
    id128 = np.eye(128, dtype=np.float32)
    maps2 = []
    for c2 in range(NCORES):
        sl = slice(256 * c2, 256 * c2 + 256)
        maps2.append(dict(
            kd=np.ascontiguousarray(kn[:, :, :, sl].transpose(2, 0, 3, 1)).reshape(64, 8 * 4096),
            qd=np.ascontiguousarray(qn[:, :, :, sl].transpose(2, 0, 3, 1)).reshape(64, 8 * 4096),
            vp=np.ascontiguousarray(vv[:, :, :, sl].transpose(3, 1, 0, 2)).reshape(4096, 512),
            bp=np.ascontiguousarray(bb_[:, :, sl].transpose(2, 1, 0)).reshape(4096, 8),
            ohg=ohg, id128=id128))
    nc2 = build_l2()
    r2 = run_bass_kernel_spmd(nc2, maps2, core_ids=list(range(NCORES)))
    lo = np.zeros((8, 1024, 2048), np.float32)
    for c2 in range(NCORES):
        blk = np.asarray(r2.results[c2]["op"]).reshape(256, 16, 8, 64).transpose(2, 1, 3, 0).reshape(8, 1024, 256)
        lo[:, :, 256 * c2:256 * c2 + 256] = blk
    maps3 = []
    for core in range(NCORES):
        m = dict(base)
        for k in WKEYS:
            m[k] = w[k]
        m["w_ob"] = w["w_ob"]
        m["x3_in"] = ox3[core]
        m["h3_in"] = oh3[core]
        m["lo_in"] = np.ascontiguousarray(lo[core])
        m["x_t"] = maps1[core]["x_t"]
        maps3.append(m)
    nc3 = build_l3()
    r3 = run_bass_kernel_spmd(nc3, maps3, core_ids=list(range(NCORES)))
    return np.stack([np.asarray(r3.results[s]["o_x"]).T for s in range(NCORES)]).astype(np.float32)


def dev_inputs(inputs, core):
    w = host_prep(inputs)
    c = host_consts()
    x = np.ascontiguousarray(np.asarray(inputs["L0_attn_a_x"])[core].T.astype(np.float32))
    m = {"x_t": x}
    for name, hk, shape, dt in CONST_SPECS:
        m["c_" + name] = w[hk] if hk in w else c[hk]
    m["ropeC"], m["ropeS"], m["identf"] = c["ropeC"], c["ropeS"], np.eye(128, dtype=np.float32)
    for k in ("w_in", "w_out", "w_qkv", "w_cproj", "w_key", "w_rec", "w_val",
              "Wq", "Wk", "Wv", "Wo", "Wg", "W1", "W2", "w_qkvb"):
        m[k] = w[k]
    return m
